# version 03
# baseline (speedup 1.0000x reference)
import os
import dataclasses
import numpy as np
import ml_dtypes
import concourse.bass as bass
import concourse.mybir as mybir
from concourse.bass_utils import run_bass_kernel_spmd
import concourse.bass_utils as _bass_utils

_orig_pnp = _bass_utils._process_ntff_profile
def _gc_pnp(*a, **kw):
    import gc
    gc.collect()
    return _orig_pnp(*a, **kw)
_bass_utils._process_ntff_profile = _gc_pnp
from concourse.expressions_rust import RuntimeValue as RV

F32 = mybir.dt.float32
BF16 = mybir.dt.bfloat16
U32 = mybir.dt.uint32
AF = mybir.ActivationFunctionType
ALU = mybir.AluOpType
AX = mybir.AxisListType
ds = bass.ds
BF = ml_dtypes.bfloat16

P = 128
D = 2048
NT = 16
H = 16
NE = 16
FF = 1408
NJ = FF // P
EPS = 1e-5
RS128 = float(1.0 / np.sqrt(128.0))
C0_KV = float(128.0 ** -1.5)
NEGBIG = -1.0e30


class Sem:
    def __init__(self, nc, name):
        self.h = nc.alloc_semaphore(name)
        self.n = 0
    def inc(self, k):
        self.n += k
        return (self, self.n)
    def tok(self):
        return (self, self.n)


class Builder:
    def __init__(self, nc):
        self.nc = nc
        self.ops = {k: [] for k in ("t", "v", "a", "g", "y")}
        self.last = {k: {} for k in ("t", "v", "a", "g", "y")}
        self.sT = Sem(nc, "sT")
        self.sV = Sem(nc, "sV")
        self.sA = Sem(nc, "sA")
        self.dsem = [Sem(nc, f"sd{i}") for i in range(30)]
        self.spoll = Sem(nc, "spoll")
        self.bwr = [None] * 8
        self.brd = [None] * 8

    def wait(self, key, tok):
        if tok is None:
            return
        sem, val = tok
        if val <= 0 or self.last[key].get(sem, -1) >= val:
            return
        self.last[key][sem] = val
        self.ops[key].append(lambda e, s=sem.h, v=val: e.wait_ge(s, v))

    def waits(self, key, toks):
        for t in toks:
            if t is not None:
                self.wait(key, t)

    def mm(self, bk, out_ap, pairs, waits=(), first=True, last=True):
        self.waits("t", list(waits) + ([self.brd[bk]] if first else []))
        n = len(pairs)
        for i, (l, r) in enumerate(pairs):
            st = first and (i == 0)
            sp = last and (i == n - 1)
            if sp:
                self.ops["t"].append(lambda e, o=out_ap, l=l, r=r, st=st:
                                     e.matmul(o, l, r, start=st, stop=True).then_inc(self.sT.h, 1))
            else:
                self.ops["t"].append(lambda e, o=out_ap, l=l, r=r, st=st, sp=sp:
                                     e.matmul(o, l, r, start=st, stop=sp))
        if last:
            tok = self.sT.inc(1)
            self.bwr[bk] = tok
            return tok
        return None

    def tr8(self, bk, src_list, dst_views, waits=()):
        self.waits("t", list(waits) + [self.brd[bk]])
        n = len(src_list)
        for i in range(n):
            if i == n - 1:
                self.ops["t"].append(lambda e, o=dst_views[i], s=src_list[i]:
                                     e.transpose(o, s, self.ident[:]).then_inc(self.sT.h, 1))
            else:
                self.ops["t"].append(lambda e, o=dst_views[i], s=src_list[i]:
                                     e.transpose(o, s, self.ident[:]))
        tok = self.sT.inc(1)
        self.bwr[bk] = tok
        return tok

    def v(self, fn, waits=(), reads=(), inc=True):
        if reads:
            inc = True
        self.waits("v", list(waits) + [self.bwr[b] for b in reads])
        if inc:
            self.ops["v"].append(lambda e, f=fn: f(e).then_inc(self.sV.h, 1))
            tok = self.sV.inc(1)
            for b in reads:
                self.brd[b] = tok
            return tok
        self.ops["v"].append(lambda e, f=fn: f(e))
        return None

    def a(self, fn, waits=(), reads=(), inc=True):
        if reads:
            inc = True
        self.waits("a", list(waits) + [self.bwr[b] for b in reads])
        if inc:
            self.ops["a"].append(lambda e, f=fn: f(e).then_inc(self.sA.h, 1))
            tok = self.sA.inc(1)
            for b in reads:
                self.brd[b] = tok
            return tok
        self.ops["a"].append(lambda e, f=fn: f(e))
        return None

    def vgap(self):
        self.ops["v"].append(lambda e: e.engine_nop())
        self.ops["v"].append(lambda e: e.engine_nop())

    def agap(self):
        self.ops["a"].append(lambda e: e.activation(self.adummy, self.adummy, AF.Identity))
        self.ops["a"].append(lambda e: e.activation(self.adummy, self.adummy, AF.Identity))

    def gd(self, dst, src, slot, waits=(), slow=False):
        self.waits("g", waits)
        sem = self.dsem[slot]
        self.ops["g"].append(lambda e, d=dst, s=src, sl=slow:
                             e.dma_start(d, s, allow_slow_non_contiguous=sl).then_inc(sem.h, 16))
        return sem.inc(16)

    def yd(self, dst, src, slot, waits=()):
        self.waits("y", waits)
        sem = self.dsem[slot]
        self.ops["y"].append(lambda e, d=dst, s=src: e.dma_start(d, s).then_inc(sem.h, 16))
        return sem.inc(16)


OFF_TABLE = []


def build_nc(subs=(1, 2, 3, 4), attnb_stage=4, dbg=False):
    OFF_TABLE.clear()
    nc = bass.Bass("TRN2", target_bir_lowering=False)
    B = Builder(nc)

    def din(name, shape, dt=F32):
        return nc.dram_tensor(name, list(shape), dt, kind="ExternalInput").ap()

    x_loc = din("x_loc", [2048, D])
    x_rem = din("x_rem", [2048, D])
    meta = din("meta", [1, 16], U32)
    fm_in = din("fm_in", [1, 4], BF16)
    offt_d = din("offt", [1, 192], U32)
    wqT = din("wqT", [D, D])
    wkvT = din("wkvT", [D, D])
    woT = din("woT", [D, D])
    wqkv = din("wqkv", [D, 3 * D])
    bqkv_d = din("bqkv", [3 * D])
    wout = din("wout", [D, D])
    bout_d = din("bout", [D])
    gateT = din("gateT", [D, NE])
    w13T = din("w13T", [NE, D, 2 * FF])
    w2T = din("w2T", [NE, FF, D])
    sw13T = din("sw13T", [D, 4 * FF])
    sw2T = din("sw2T", [2 * FF, D])
    y_out = nc.dram_tensor("y", [2048, D], F32, kind="ExternalOutput").ap()
    dbg_out = nc.dram_tensor("dbg", [12, P, 512], F32, kind="ExternalOutput").ap() if dbg else None

    qt_scr = nc.dram_tensor("qt_scr", [D, 2048], BF16, kind="Internal").ap()
    k_loc = nc.dram_tensor("k_loc", [D, 2048], BF16, kind="Internal").ap()
    v_loc = nc.dram_tensor("v_loc", [16, P, D], BF16, kind="Internal").ap()
    sh_k = nc.dram_tensor("sh_k", [2, D, 2048], BF16, kind="Internal", addr_space="Shared").ap()
    sh_v = nc.dram_tensor("sh_v", [2, 16, P, D], BF16, kind="Internal", addr_space="Shared").ap()
    sh_aux = nc.dram_tensor("sh_aux", [2, 16], F32, kind="Internal", addr_space="Shared").ap()
    sh_flag = nc.dram_tensor("sh_flag", [2, 16], U32, kind="Internal", addr_space="Shared").ap()

    r_ = np.arange(P)
    ang = 2.0 * np.pi * np.outer(r_, r_) / 128.0
    u2_np = np.zeros((P, 2, 256), dtype=BF)
    for j in range(2):
        for r in range(P):
            u2_np[r, j, (128 * j + r):] = 1.0
    c_idn = nc.inline_tensor(np.eye(P, dtype=BF), "c_idn")
    c_dftC = nc.inline_tensor(np.cos(ang).astype(BF), "c_dftC")
    c_dftS = nc.inline_tensor(np.sin(ang).astype(BF), "c_dftS")
    c_u2 = nc.inline_tensor(u2_np, "c_u2")
    c_onec = nc.inline_tensor(np.ones((P, 1), dtype=BF), "c_onec")
    c_eps = nc.inline_tensor(np.full((P, 1), EPS, dtype=np.float32), "c_eps")
    c_oner = nc.inline_tensor(np.ones((1, P), dtype=BF), "c_oner")

    Al = nc.alloc_sbuf_tensor
    X = Al("X", [P, NT, D], F32)
    ident = Al("ident", [P, P], BF16)
    B.ident = ident
    onec = Al("onec", [P, 1], BF16)
    oner = Al("oner", [1, P], BF16)
    gate_sb = Al("gate_sb", [P, 16, NE], BF16)
    bqkv_sb = Al("bqkv_sb", [P, 48], F32)
    bqkv_sc = Al("bqkv_sc", [P, 48], F32)
    ms = Al("ms", [1, 16], U32)
    flg = Al("flg", [1, 16], U32)
    fmb = Al("fmb", [1, 4], BF16)
    offt = Al("offt_sb", [1, 192], U32)
    fmcol = Al("fmcol", [P, 4], F32)
    stats = Al("stats", [P, 88], F32)
    epsc = Al("epsc", [P, 1], F32)
    adummy = Al("adummy", [1, 8], F32)
    B.adummy = adummy[0:1, 0:8]
    srow = Al("srow", [1, 96], F32)
    ARN = 39168
    arn = Al("arn", [P, ARN], BF16)

    def abf(off, shape):
        n = int(np.prod(shape))
        assert 0 <= off and off + n <= ARN, (off, shape)
        v = arn[:, off:off + n]
        if len(shape) > 1:
            names = " ".join(f"d{i}" for i in range(len(shape)))
            v = v.rearrange(f"p ({names}) -> p {names}",
                            **{f"d{i}": s for i, s in enumerate(shape)})
        return v

    def af32(off, shape):
        n = int(np.prod(shape))
        assert 0 <= off and off + 2 * n <= ARN, (off, shape)
        v = arn[:, off:off + 2 * n].bitcast(F32)
        if len(shape) > 1:
            names = " ".join(f"d{i}" for i in range(len(shape)))
            v = v.rearrange(f"p ({names}) -> p {names}",
                            **{f"d{i}": s for i, s in enumerate(shape)})
        return v

    def arow_bf(off, n):
        return arn[0:1, off:off + n]

    pb = [nc.alloc_psum_tensor(f"pb{i}", [P, 512], F32) for i in range(8)]
    def pbf_views(bk, n=8):
        v = pb[bk][:].bitcast(BF16).rearrange("p (c n) -> p c n", n=P)
        return [v[:, i, :] for i in range(n)]
    def pbf3(bk):
        return pb[bk][:].bitcast(BF16).rearrange("p (c n) -> p c n", n=P)

    tk_con = []
    tk_con.append(B.gd(ident[:], c_idn.ap(), 0))
    tk_con.append(B.gd(onec[:], c_onec.ap(), 0))
    tk_con.append(B.gd(oner[:], c_oner.ap(), 0))
    tk_con.append(B.gd(gate_sb[:], gateT.rearrange("(c p) n -> p c n", p=P), 0))
    tk_con.append(B.gd(bqkv_sb[:], bqkv_d.rearrange("(c p) -> p c", p=P), 0, slow=True))
    tk_con.append(B.gd(ms[:], meta, 0))
    tk_con.append(B.gd(fmb[:], fm_in, 0))
    tk_con.append(B.gd(epsc[:], c_eps.ap(), 0))
    tk_con.append(B.gd(offt[:], offt_d, 0))
    tk_con = [B.dsem[0].tok()]
    tk_x = B.yd(X[:], x_loc.rearrange("(t p) d -> p t d", p=P), 1)
    tkf = B.mm(7, pb[7][:, 0:4], [(oner[:], fmb[:])], waits=tk_con)
    B.v(lambda e: e.tensor_copy(fmcol[:], pb[7][:, 0:4]), waits=[tkf], reads=[7])
    tk_fm = B.sV.tok()
    B.v(lambda e: e.tensor_copy(bqkv_sc[:], bqkv_sb[:]), waits=tk_con, inc=False)
    for h in range(H):
        B.v(lambda e, h=h: e.tensor_scalar(bqkv_sc[:, 3 * h:3 * h + 1], bqkv_sb[:, 3 * h:3 * h + 1],
                                           RS128, None, ALU.mult), inc=(h == H - 1))
    tk_bq = B.sV.tok()

    tk_X = [None] * NT

    def emit_ln(n, src, lnxT, scr0, scr1, tr_extra=()):
        s1 = stats[:, 0:n];        s2 = stats[:, n:2 * n]
        mn = stats[:, 2 * n:3 * n];  ex2 = stats[:, 3 * n:4 * n]
        m2 = stats[:, 4 * n:5 * n];  var = stats[:, 5 * n:6 * n]
        srt = stats[:, 6 * n:7 * n]; rstd = stats[:, 7 * n:8 * n]
        nb = stats[:, 8 * n:9 * n]
        dump = [scr0, scr1]
        aps, tks = [], []
        for i in range(n):
            ap, tk = src(i)
            aps.append(ap); tks.append(tk)
            B.v(lambda e, a=ap, o=s1[:, i:i + 1]: e.reduce_sum(o, a, axis=AX.X),
                waits=tk, inc=(i == n - 1))
        tk_s1 = B.sV.tok()
        for i in range(n):
            B.a(lambda e, a=aps[i], o=dump[i % 2], s=s2[:, i:i + 1]:
                e.activation(o, a, AF.Square, accum_out=s),
                waits=list(tks[i]) + list(tr_extra), inc=(i == n - 1))
        tk_s2 = B.sA.tok()
        B.v(lambda e: e.tensor_scalar(ex2[:], s2[:], 1.0 / D, None, ALU.mult),
            waits=[tk_s2], inc=False)
        B.v(lambda e: e.tensor_scalar(mn[:], s1[:], 1.0 / D, None, ALU.mult), waits=[tk_s1])
        tk_mn = B.sV.tok()
        tk_m2 = B.a(lambda e: e.activation(m2[:], mn[:], AF.Square), waits=[tk_mn])
        B.v(lambda e: e.tensor_sub(var[:], ex2[:], m2[:]), waits=[tk_m2])
        tk_var = B.sV.tok()
        tk_srt = B.a(lambda e: e.activation(srt[:], var[:], AF.Sqrt, bias=epsc[:, 0:1]),
                     waits=[tk_var] + tk_con)
        B.v(lambda e: e.reciprocal(rstd[:], srt[:]), waits=[tk_srt], inc=False)
        B.vgap()
        B.v(lambda e: e.scalar_tensor_tensor(nb[:], mn[:], -1.0, rstd[:], ALU.mult, ALU.mult))
        tk_nb = B.sV.tok()
        tk_vcp = None
        for i in range(n):
            lnx = dump[i % 2]
            tk_ap = B.a(lambda e, a=aps[i], o=lnx, i=i:
                        e.activation(o, a, AF.Identity, scale=rstd[:, i:i + 1], bias=nb[:, i:i + 1]),
                        waits=[tk_nb] + ([tk_vcp] if tk_vcp else []))
            lx2 = lnx.rearrange("p (c n) -> p c n", n=P)
            for hf in range(2):
                bank = 6 + hf
                tk_tr = B.tr8(bank, [lx2[:, hf * 8 + c, :] for c in range(8)], pbf_views(bank),
                              waits=[tk_ap] + list(tr_extra))
                tk_vcp = B.v(lambda e, b=bank, i=i, hf=hf: e.tensor_copy(
                    lnxT[:, hf * 8:(hf + 1) * 8, i * P:(i + 1) * P], pbf3(b)),
                    waits=[tk_tr], reads=[bank])
        return tk_vcp

    def xsrc_mk(gt):
        def f(i):
            t = gt[i]
            return (X[:, t, :], [tk_x, tk_X[t]])
        return f

    def emit_attn_a():
        o_lnxT, o_qnT, o_kvn = 0, 4096, 8192
        o_raw, o_sq, o_qn = 12288, 16384, 18432
        o_w = 22528
        o_u2, o_dC, o_dS = 30720, 31232, 31360
        o_cum, o_e = 31488, 32100

        lnxT = abf(o_lnxT, (16, 256))
        qnT = abf(o_qnT, (16, 256))
        kvn = abf(o_kvn, (2, 2048))
        raw = [abf(o_raw, (2048,)), abf(o_raw + 2048, (2048,))]
        sqb = abf(o_sq, (2048,))
        qnb = [abf(o_qn, (2048,)), abf(o_qn + 2048, (2048,))]
        wsl = [abf(o_w, (16, 256)), abf(o_w + 4096, (16, 256))]
        u2 = abf(o_u2, (2, 256))
        dC = abf(o_dC, (P,))
        dS = abf(o_dS, (P,))
        cums = [abf(o_cum, (256,)), abf(o_cum + 256, (256,))]
        kcs = abf(o_e, (256,)); kss = abf(o_e + 256, (256,))
        e1s = abf(o_e + 512, (256,)); e2s = abf(o_e + 768, (256,))
        t14 = [af32(o_e + 1024 + i * 512, (256,)) for i in range(4)]
        xr = [af32(4096, (2048,)), af32(18432, (2048,))]

        tk_cst = [B.gd(u2, c_u2.ap(), 2), B.gd(dC, c_dftC.ap(), 2), B.gd(dS, c_dftS.ap(), 2)]

        carry = [stats[:, 18 + 16 * i:34 + 16 * i] for i in range(2)]
        s2h = stats[:, 50:66]
        rch = stats[:, 66:82]
        rch_b = bass.AP(rch.tensor, rch.offset, [rch.ap[0], [rch.ap[1][0], 16], [0, P]])

        wtok = [None, None]

        def proj_norm(WT, tk_ln, c0, dst, pre=()):
            tk_ev = [None, None]
            for c in range(8):
                w = wsl[c % 2]
                tk_w = B.gd(w, WT[:, c * 256:(c + 1) * 256].rearrange("(k p) n -> p k n", p=P),
                            3 if c % 2 == 0 else 9,
                            waits=[wtok[c % 2]] if wtok[c % 2] else ())
                for i in range(2):
                    bank = (c * 2 + i) % 2
                    tkm = B.mm(bank, pb[bank][:, 0:256],
                               [(lnxT[:, k, i * P:(i + 1) * P], w[:, k, :]) for k in range(16)],
                               waits=[tk_w, tk_ln])
                    tk_ev[i] = B.a(lambda e, b=bank, i=i, c=c:
                                   e.activation(raw[i][:, c * 256:(c + 1) * 256],
                                                pb[b][:, 0:256], AF.Identity),
                                   waits=[tkm] + (list(pre) if c == 0 else []), reads=[bank])
                wtok[c % 2] = B.sT.tok()
            out = []
            for i in range(2):
                B.agap()
                tk_sq = B.a(lambda e, i=i: e.activation(sqb, raw[i], AF.Square), waits=[tk_ev[i]])
                B.v(lambda e: e.reduce_sum(s2h[:], sqb.rearrange("p (h n) -> p h n", n=P),
                                           axis=AX.X), waits=[tk_sq])
                tk_rt = B.a(lambda e: e.activation(rch[:], s2h[:], AF.Sqrt), waits=[B.sV.tok()])
                B.v(lambda e: e.reciprocal(rch[:], rch[:]), waits=[tk_rt], inc=False)
                B.vgap()
                B.v(lambda e, i=i, d=dst[i]: e.scalar_tensor_tensor(
                    d.rearrange("p (h n) -> p h n", n=P),
                    raw[i].rearrange("p (h n) -> p h n", n=P), c0, rch_b, ALU.mult, ALU.mult))
                out.append(B.sV.tok())
            return out

        tk_sum = None
        first = True
        for g in range(8):
            tk_src = []
            for i in range(2):
                t = 2 * g + i
                tk_src.append([B.yd(xr[i][:], x_rem.rearrange("(t p) d -> p t d", p=P)[:, t, :],
                                    4 if i == 0 else 15,
                                    waits=[tk_sum] if tk_sum else ())])
            tk_ln = emit_ln(2, lambda i: (xr[i], tk_src[i]), lnxT, raw[0], raw[1],
                            tr_extra=tk_cst if g == 0 else ())
            tk_kvn = proj_norm(wkvT, tk_ln, C0_KV, [kvn[:, 0, :], kvn[:, 1, :]])
            for i in range(2):
                for h in range(H):
                    B.mm(5, pb[5][:, h:h + 1],
                         [(kvn[:, i, :].rearrange("p (hh n) -> p hh n", n=P)[:, h, :], onec[:])],
                         waits=[tk_kvn[i]] if h == 0 else (),
                         first=first, last=(g == 7 and i == 1 and h == H - 1))
                    first = False
            tk_sum = B.sT.tok()
        B.v(lambda e: e.tensor_scalar(carry[0][:], pb[5][:, 0:16], fmcol[:, 0:1], None, ALU.mult),
            waits=[tk_sum, tk_fm], reads=[5])
        tk_carry = B.sV.tok()

        tk_prev_grp = None
        for g in range(8):
            gt = [2 * g, 2 * g + 1]
            cpar = g % 2
            tk_ln = emit_ln(2, xsrc_mk(gt), lnxT, raw[0], raw[1],
                            tr_extra=[tk_prev_grp] if tk_prev_grp else ())
            tk_qn = proj_norm(wqT, tk_ln, RS128, [qnb[0], qnb[1]])
            tk_qT = None
            for i in range(2):
                q2 = qnb[i].rearrange("p (c n) -> p c n", n=P)
                for hf in range(2):
                    bank = 6 + hf
                    tk_tr = B.tr8(bank, [q2[:, hf * 8 + c, :] for c in range(8)], pbf_views(bank),
                                  waits=[tk_qn[i]])
                    tk_qT = B.v(lambda e, b=bank, i=i, hf=hf: e.tensor_copy(
                        qnT[:, hf * 8:(hf + 1) * 8, i * P:(i + 1) * P], pbf3(b)),
                        waits=[tk_tr], reads=[bank])
            tk_kvn = proj_norm(wkvT, tk_ln, C0_KV, [kvn[:, 0, :], kvn[:, 1, :]], pre=tk_qn)
            vhT = lnxT
            tk_vh = None
            tk_cu_use = [None, None]
            tk_e_prev = None
            for h in range(H):
                kv_h = [kvn[:, j, :].rearrange("p (hh n) -> p hh n", n=P)[:, h, :] for j in range(2)]
                tkc = B.mm(2, pb[2][:, 0:256], [(kv_h[j], u2[:, j, :]) for j in range(2)],
                           waits=[tk_kvn[0], tk_kvn[1], tk_carry])
                cu = cums[h % 2]
                tk_cume = B.a(lambda e, cu=cu, h=h, cp=cpar:
                              e.activation(cu, pb[2][:, 0:256], AF.Identity,
                                           bias=carry[cp][:, h:h + 1]),
                              waits=[tkc] + ([tk_cu_use[h % 2]] if tk_cu_use[h % 2] else []),
                              reads=[2])
                B.v(lambda e, h=h, cp=cpar: e.tensor_tensor(
                    carry[1 - cp][:, h:h + 1], pb[2][:, 255:256], carry[cp][:, h:h + 1], ALU.add),
                    waits=[tkc, tk_cume], reads=[2])
                tkq1 = B.mm(0, pb[0][:, 0:256], [(dC, qnT[:, h, :])], waits=[tk_qT])
                tkq2 = B.mm(1, pb[1][:, 0:256], [(dS, qnT[:, h, :])])
                tkk1 = B.mm(3, pb[3][:, 0:256], [(dC, cu)], waits=[tk_cume])
                tkk2 = B.mm(4, pb[4][:, 0:256], [(dS, cu)])
                tk_cu_use[h % 2] = tkk2
                tk_kc = B.a(lambda e: e.activation(kcs, pb[3][:, 0:256], AF.Identity),
                            waits=[tkk1] + ([tk_e_prev] if tk_e_prev else []), reads=[3])
                tk_ks = B.a(lambda e: e.activation(kss, pb[4][:, 0:256], AF.Identity),
                            waits=[tkk2], reads=[4])
                B.v(lambda e: e.tensor_mul(t14[0][:], pb[0][:, 0:256], kcs),
                    waits=[tkq1, tk_kc], inc=False)
                B.v(lambda e: e.tensor_mul(t14[1][:], pb[1][:, 0:256], kss),
                    waits=[tkq2, tk_ks], inc=False)
                B.v(lambda e: e.tensor_mul(t14[2][:], pb[0][:, 0:256], kss), inc=False)
                B.v(lambda e: e.tensor_mul(t14[3][:], pb[1][:, 0:256], kcs), inc=False)
                B.v(lambda e: e.tensor_sub(e1s, t14[0][:], t14[1][:]), inc=False)
                B.v(lambda e: e.tensor_add(e2s, t14[2][:], t14[3][:]))
                tk_e = B.sV.tok()
                tk_e_prev = tk_e
                B.brd[0] = tk_e; B.brd[1] = tk_e
                tk_vhm = B.mm(5, pb[5][:, 0:256], [(dC, e1s), (dS, e2s)], waits=[tk_e])
                tk_vh = B.v(lambda e, h=h, vhT=vhT: e.tensor_copy(vhT[:, h, :], pb[5][:, 0:256]),
                            waits=[tk_vhm], reads=[5])
            for c in range(8):
                w = wsl[c % 2]
                tk_w = B.gd(w, woT[:, c * 256:(c + 1) * 256].rearrange("(k p) n -> p k n", p=P),
                            3 if c % 2 == 0 else 9,
                            waits=[wtok[c % 2]] if wtok[c % 2] else ())
                for i in range(2):
                    bank = (c * 2 + i) % 2
                    tkm = B.mm(bank, pb[bank][:, 0:256],
                               [(vhT[:, k, i * P:(i + 1) * P], w[:, k, :]) for k in range(16)],
                               waits=[tk_w, tk_vh])
                    t = gt[i]
                    tk_X[t] = B.v(lambda e, b=bank, t=t, c=c: e.tensor_tensor(
                        X[:, t, c * 256:(c + 1) * 256], X[:, t, c * 256:(c + 1) * 256],
                        pb[b][:, 0:256], ALU.add), waits=[tkm, tk_X[t]], reads=[bank])
                wtok[c % 2] = B.sT.tok()
            tk_prev_grp = B.sT.tok()

    ffn_state = {"tk_h_prev": None, "tk_w2_slot": [None, None], "tk_w13_slot": [None, None]}

    def emit_ffn():
        o_lnxT, o_cg, o_hT = 0, 16384, 16640
        o_wk, o_sil, o_wA = 27904, 28416, 29440
        o_w2s1, o_wB = 27904, 33536
        lnxT = abf(o_lnxT, (16, 1024))
        wk = [af32(o_wk, (8, 16)), af32(o_wk + 256, (8, 16))]
        cg = af32(o_cg, (8, 16))
        sil = [abf(o_sil, (512,)), abf(o_sil + 512, (512,))]
        hT = abf(o_hT, (NJ, 1024))
        w13s = [abf(o_wA, (16, P)), abf(o_wA + 2048, (16, P))]
        w2s = [abf(o_wB, (NJ, 512)), abf(o_w2s1, (NJ, 512))]
        scr0 = abf(o_hT, (2048,))
        scr1 = abf(o_hT + 2048, (2048,))
        gth = stats[:, 72:80]
        gtmp = stats[:, 80:88]

        ent = B.sT.tok()
        tk_h_prev = ent
        tk_w2_slot = [ent, ent]
        tk_w13_slot = [ent, ent]

        for grp in range(2):
            gt = list(range(grp * 8, grp * 8 + 8))
            tk_ln = emit_ln(8, xsrc_mk(gt), lnxT, scr0, scr1,
                            tr_extra=[tk_h_prev] if tk_h_prev else ())
            for i in range(8):
                bank = 6 + i % 2
                tkg = B.mm(bank, pb[bank][:, 0:16],
                           [(lnxT[:, k, i * P:(i + 1) * P], gate_sb[:, k, :]) for k in range(16)],
                           waits=[tk_ln])
                B.v(lambda e, b=bank, i=i: e.reduce_max(gtmp[:, i:i + 1], pb[b][:, 0:16],
                                                        axis=AX.X, negate=True),
                    waits=[tkg], inc=False)
                B.vgap()
                B.v(lambda e, b=bank, i=i: e.tensor_scalar(wk[0][:, i, :], pb[b][:, 0:16],
                                                           gtmp[:, i:i + 1], None, ALU.add),
                    reads=[bank])
                tk_sc = B.sV.tok()
                B.a(lambda e, i=i: e.activation(wk[1][:, i, :], wk[0][:, i, :], AF.Exp),
                    waits=[tk_sc], inc=(i == 7))
            tk_exp = B.sA.tok()
            for i in range(8):
                B.v(lambda e, i=i: e.reduce_sum(gtmp[:, i:i + 1], wk[1][:, i, :], axis=AX.X),
                    waits=[tk_exp] if i == 0 else (), inc=(i == 7))
            for i in range(8):
                B.v(lambda e, i=i: e.reciprocal(gth[:, i:i + 1], gtmp[:, i:i + 1]), inc=(i == 7))
            for i in range(8):
                B.v(lambda e, i=i: e.tensor_scalar(wk[1][:, i, :], wk[1][:, i, :], gth[:, i:i + 1],
                                                   None, ALU.mult), inc=(i == 7))
            for i in range(8):
                B.v(lambda e, i=i: e.tensor_copy(wk[0][:, i, :], wk[1][:, i, :]), inc=(i == 7))
            for it in range(6):
                for i in range(8):
                    B.v(lambda e, i=i: e.reduce_max(gtmp[:, i:i + 1], wk[0][:, i, :], axis=AX.X),
                        inc=(i == 7))
                if it == 5:
                    break
                for i in range(8):
                    B.v(lambda e, i=i: e.tensor_scalar(cg[:, i, :], wk[0][:, i, :],
                                                       gtmp[:, i:i + 1], None, ALU.is_ge),
                        inc=(i == 7))
                for i in range(8):
                    B.v(lambda e, i=i: e.scalar_tensor_tensor(wk[0][:, i, :], cg[:, i, :], NEGBIG,
                                                              wk[0][:, i, :], ALU.mult, ALU.add),
                        inc=(i == 7))
            for i in range(8):
                B.v(lambda e, i=i: e.tensor_scalar(cg[:, i, :], wk[1][:, i, :], gtmp[:, i:i + 1],
                                                   None, ALU.is_ge), inc=(i == 7))
            for i in range(8):
                B.v(lambda e, i=i: e.tensor_mul(cg[:, i, :], cg[:, i, :], wk[1][:, i, :]),
                    inc=(i == 7))
            for i in range(8):
                B.v(lambda e, i=i: e.reduce_sum(gth[:, i:i + 1], cg[:, i, :], axis=AX.X),
                    inc=(i == 7))
            for i in range(8):
                B.v(lambda e, i=i: e.tensor_scalar(gtmp[:, i:i + 1], gth[:, i:i + 1], 1e-20, None,
                                                   ALU.add), inc=(i == 7))
            for i in range(8):
                B.v(lambda e, i=i: e.reciprocal(gth[:, i:i + 1], gtmp[:, i:i + 1]), inc=(i == 7))
            for i in range(8):
                B.v(lambda e, i=i: e.tensor_scalar(cg[:, i, :], cg[:, i, :], gth[:, i:i + 1],
                                                   None, ALU.mult), inc=(i == 7))
            tk_cg = B.sV.tok()

            for ep in range(NE + 2):
                if ep < NE:
                    w13g = lambda j, ep=ep: w13T[ep, :, j * P:(j + 1) * P]
                    w13u = lambda j, ep=ep: w13T[ep, :, FF + j * P:FF + (j + 1) * P]
                    w2src = lambda dc, ep=ep: w2T[ep, :, dc * 512:(dc + 1) * 512]
                    def c_ap(tt, ep=ep):
                        return cg[:, tt, ep:ep + 1]
                else:
                    sp = ep - NE
                    w13g = lambda j, sp=sp: sw13T[:, sp * FF + j * P:sp * FF + (j + 1) * P]
                    w13u = lambda j, sp=sp: sw13T[:, 2 * FF + sp * FF + j * P:2 * FF + sp * FF + (j + 1) * P]
                    w2src = lambda dc, sp=sp: sw2T[sp * FF:(sp + 1) * FF, dc * 512:(dc + 1) * 512]
                    c_ap = None
                tk_h = [None] * NJ
                tk_hmul = [None, None]
                for j in range(NJ):
                    tk_g = B.gd(w13s[0], w13g(j).rearrange("(k p) n -> p k n", p=P), 5,
                                waits=[tk_w13_slot[0]] if tk_w13_slot[0] else ())
                    tk_u = B.gd(w13s[1], w13u(j).rearrange("(k p) n -> p k n", p=P), 6,
                                waits=[tk_w13_slot[1]] if tk_w13_slot[1] else ())
                    for hf in range(2):
                        tkg = B.mm(hf, pb[hf][:, :],
                                   [(w13s[0][:, k, :], lnxT[:, k, hf * 512:(hf + 1) * 512])
                                    for k in range(16)], waits=[tk_g, tk_ln])
                        tku = B.mm(2 + hf, pb[2 + hf][:, :],
                                   [(w13s[1][:, k, :], lnxT[:, k, hf * 512:(hf + 1) * 512])
                                    for k in range(16)], waits=[tk_u])
                        tk_si = B.a(lambda e, hf=hf: e.activation(sil[hf], pb[hf][:, :], AF.Silu),
                                    waits=[tkg] + ([tk_hmul[hf]] if tk_hmul[hf] else []),
                                    reads=[hf])
                        tk_hmul[hf] = B.v(lambda e, j=j, hf=hf: e.tensor_mul(
                            hT[:, j, hf * 512:(hf + 1) * 512], sil[hf], pb[2 + hf][:, :]),
                            waits=[tk_si, tku] + ([tk_h_prev] if tk_h_prev else []),
                            reads=[2 + hf])
                        tk_h[j] = tk_hmul[hf]
                    tk_w13_slot = [B.sT.tok(), B.sT.tok()]
                tk_w2_slot[0] = B.gd(w2s[0], w2src(0).rearrange("(j p) n -> p j n", p=P), 7,
                                     waits=[tk_w2_slot[0]] if tk_w2_slot[0] else ())
                for dc in range(4):
                    sl = dc % 2
                    if dc > 0:
                        w = [tk_w2_slot[sl]] if tk_w2_slot[sl] else []
                        if sl == 1 and dc == 1:
                            w += [tk_w13_slot[0], tk_w13_slot[1]]
                        tk_w2_slot[sl] = B.gd(w2s[sl], w2src(dc).rearrange("(j p) n -> p j n", p=P),
                                              7 + sl, waits=w)
                    for tt in range(8):
                        bank = 4 + tt % 2
                        tkb = B.mm(bank, pb[bank][:, :],
                                   [(hT[:, j, tt * P:(tt + 1) * P], w2s[sl][:, j, :])
                                    for j in range(NJ)],
                                   waits=[tk_w2_slot[sl]] + (tk_h if (dc == 0 and tt == 0) else []))
                        t = gt[tt]
                        if c_ap is None:
                            tk_X[t] = B.v(lambda e, b=bank, t=t, dc=dc: e.tensor_tensor(
                                X[:, t, dc * 512:(dc + 1) * 512], X[:, t, dc * 512:(dc + 1) * 512],
                                pb[b][:, :], ALU.add), waits=[tkb, tk_X[t]], reads=[bank])
                        else:
                            tk_X[t] = B.v(lambda e, b=bank, t=t, dc=dc, ca=c_ap(tt):
                                          e.scalar_tensor_tensor(
                                X[:, t, dc * 512:(dc + 1) * 512], pb[b][:, :], ca,
                                X[:, t, dc * 512:(dc + 1) * 512], ALU.mult, ALU.add),
                                waits=[tkb, tk_X[t], tk_cg], reads=[bank])
                    tk_w2_slot[sl] = B.sT.tok()
                tk_h_prev = B.sT.tok()
                tk_w13_slot = [tk_h_prev, tk_h_prev]
        ffn_state["tk_h_prev"] = tk_h_prev
        ffn_state["tk_w2_slot"] = tk_w2_slot
        ffn_state["tk_w13_slot"] = tk_w13_slot

    def emit_attn_b(stage=4):
        lnxT = abf(0, (16, 512))
        wqs = [abf(8192, (16, 3, P)), abf(14336, (16, 3, P))]
        qst = [abf(20480, (512,)), abf(20992, (512,))]
        kst = [abf(21504, (512,)), abf(22016, (512,))]
        ksq = abf(22528, (512,))
        vst = abf(23040, (512,))
        vts = [abf(23552, (4, P)), abf(24064, (4, P))]
        scr0 = abf(8192, (2048,))
        scr1 = abf(10240, (2048,))
        kbuf = [abf(8192, (32, P)), abf(12288, (32, P))]
        qbuf = [abf(16384, (512,)), abf(16896, (512,))]
        vbuf = [abf(17408, (32, P)), abf(21504, (32, P))]
        PT = abf(25600, (16, 512))
        attnT = abf(0, (16, 512))
        wob = [abf(25600, (16, 256)), abf(29696, (16, 256))]
        rl_f = af32(33792, (512,))
        rowbf = arow_bf(34816, 512)
        bout_row = arow_bf(35328, 2048)
        negUb = stats[:, 36:52]
        qmax2 = srow[0:1, 0:16]
        kmax2 = srow[0:1, 16:32]
        negU = srow[0:1, 32:48]
        tmpr = srow[0:1, 48:64]
        aux2 = srow[0:1, 64:96]

        B.waits("g", tk_con)

        def gd_dyn(static_ap, half_scale, src, slot, waits=(), small=False):
            B.waits("g", waits)
            sem = B.dsem[slot]
            idx = len(OFF_TABLE)
            OFF_TABLE.append((S_HS := (int(static_ap.offset), half_scale)))
            reg = nc.gpsimd.alloc_register(f"dynoff{idx}")
            rv = RV(reg, min_val=0, max_val=48 if small else 2 * 16 * P * D)
            ap2 = dataclasses.replace(static_ap, offset=rv)
            def op(e, d=ap2, s=src, reg=reg, idx=idx):
                e.reg_load(reg, offt[0:1, idx:idx + 1])
                e.dma_start(d, s).then_inc(sem.h, 16)
            B.ops["g"].append(op)
            return sem.inc(16)

        B.v(lambda e: e.memset(qmax2, 0.0), inc=False)
        B.v(lambda e: e.memset(kmax2, 0.0))
        tk_mx0 = B.sV.tok()

        tk_q_dma = tk_k_dma = tk_v_dma = None
        ent_b = B.sT.tok()
        tk_wprev = [ent_b, ent_b]
        tk_prev_use = ent_b
        for g in range(4):
            gt = list(range(g * 4, g * 4 + 4))
            tk_ln = emit_ln(4, xsrc_mk(gt), lnxT, scr0, scr1,
                            tr_extra=[tk_prev_use] if tk_prev_use else ())
            for hb in range(H):
                hp = hb % 2
                w = wqs[hp]
                tk_w = B.gd(w, wqkv[:, hb * 384:(hb + 1) * 384]
                            .rearrange("(k p) (c n) -> p k c n", p=P, n=P),
                            10 if hp == 0 else 20,
                            waits=[tk_wprev[hp], tk_ln] if tk_wprev[hp] else [tk_ln])
                tkq = B.mm(0, pb[0][:, :], [(w[:, k, 0, :], lnxT[:, k, :]) for k in range(16)],
                           waits=[tk_w, tk_ln, tk_q_dma])
                tk_qe = B.a(lambda e, hb=hb, hp=hp: e.activation(qst[hp], pb[0][:, :], AF.Identity,
                            scale=RS128, bias=bqkv_sc[:, 3 * hb:3 * hb + 1]),
                            waits=[tkq, tk_bq], reads=[0])
                B.agap()
                tk_qsq = B.a(lambda e, hp=hp: e.activation(ksq, qst[hp], AF.Square), waits=[tk_qe])
                tkqs = B.mm(3, pb[3][0:1, :], [(onec[:], ksq)], waits=[tk_qsq])
                B.v(lambda e: e.reduce_max(tmpr[0:1, 0:1], pb[3][0:1, :], axis=AX.X),
                    waits=[tkqs], inc=False)
                B.vgap()
                B.v(lambda e, hb=hb: e.tensor_max(qmax2[0:1, hb:hb + 1], qmax2[0:1, hb:hb + 1],
                                                  tmpr[0:1, 0:1]), waits=[tk_mx0], reads=[3])
                tk_q_dma = B.yd(qt_scr[hb * P:(hb + 1) * P, g * 512:(g + 1) * 512], qst[hp],
                                11 if hp == 0 else 24, waits=[B.sA.tok()])
                tkk = B.mm(1, pb[1][:, :], [(w[:, k, 1, :], lnxT[:, k, :]) for k in range(16)],
                           waits=[tk_k_dma])
                tk_ke = B.a(lambda e, hb=hb, hp=hp: e.activation(kst[hp], pb[1][:, :], AF.Identity,
                            bias=bqkv_sc[:, 3 * hb + 1:3 * hb + 2]), waits=[tkk], reads=[1])
                B.agap()
                tk_ksq = B.a(lambda e, hp=hp: e.activation(ksq, kst[hp], AF.Square), waits=[tk_ke])
                tkks = B.mm(3, pb[3][0:1, :], [(onec[:], ksq)], waits=[tk_ksq])
                B.v(lambda e: e.reduce_max(tmpr[0:1, 1:2], pb[3][0:1, :], axis=AX.X),
                    waits=[tkks], inc=False)
                B.vgap()
                B.v(lambda e, hb=hb: e.tensor_max(kmax2[0:1, hb:hb + 1], kmax2[0:1, hb:hb + 1],
                                                  tmpr[0:1, 1:2]), reads=[3])
                tk_k_dma = B.yd(k_loc[hb * P:(hb + 1) * P, g * 512:(g + 1) * 512],
                                kst[hp], 12 if hp == 0 else 25, waits=[B.sA.tok()])
                if dbg and g == 0 and hb == 0:
                    B.gd(dbg_out[8], sh_k[0, 0:128, 0:512], 2, waits=[tk_k_dma])
                    B.gd(dbg_out[9], kst[hp], 2, waits=[B.dsem[2].tok()])
                    B.gd(dbg_out[10][0:1, 0:192], offt[:], 2, waits=[B.dsem[2].tok()])
                tkv = B.mm(2, pb[2][:, :], [(w[:, k, 2, :], lnxT[:, k, :]) for k in range(16)],
                           waits=[tk_v_dma])
                tk_ve = B.a(lambda e, hb=hb: e.activation(vst, pb[2][:, :], AF.Identity,
                            bias=bqkv_sc[:, 3 * hb + 2:3 * hb + 3]), waits=[tkv], reads=[2])
                v2 = vst.rearrange("p (c n) -> p c n", n=P)
                tk_vt = B.tr8(5, [v2[:, tt, :] for tt in range(4)], pbf_views(5, 4),
                              waits=[tk_ve])
                tk_vc = B.v(lambda e, hp=hp: e.tensor_copy(vts[hp][:], pbf3(5)[:, 0:4, :]),
                            waits=[tk_vt], reads=[5])
                tk_v_dma = B.yd(
                    v_loc[g * 4:(g + 1) * 4, :, hb * P:(hb + 1) * P]
                    .rearrange("a p d -> p a d"),
                    vts[hp], 13 if hp == 0 else 26, waits=[tk_vc])
                tk_wprev[hp] = B.sT.tok()
            tk_prev_use = B.sT.tok()

        if stage <= 1:
            return
        tk_pub_k = gd_dyn(sh_k[0, :, :], D * 2048, k_loc[:, :], 12,
                          waits=[B.dsem[12].tok(), B.dsem[25].tok()])
        tk_pub_v = gd_dyn(sh_v[0, :, :, :].rearrange("a p d -> (a p) d"), 16 * P * D,
                          v_loc[:, :, :].rearrange("a p d -> (a p) d"), 13,
                          waits=[B.dsem[13].tok(), B.dsem[26].tok()])
        tk_aux = gd_dyn(sh_aux[0:1, :], 16, kmax2, 14,
                        waits=[B.sV.tok(), tk_pub_k, tk_pub_v,
                               B.dsem[11].tok(), B.dsem[24].tok()], small=True)
        B.ops["g"].append(lambda e: e.memset(flg[:], 0))
        B.ops["g"].append(lambda e: e.tensor_copy(flg[0:1, 0:2], ms[0:1, 1:3]))
        B.waits("g", [tk_aux])
        tk_flag = gd_dyn(sh_flag[0:1, :], 16, flg[0:1, :], 14, small=True)
        B.waits("g", [tk_flag])
        spoll = B.spoll

        spin_reg = nc.gpsimd.alloc_register("spinoff")
        pflag_ap = dataclasses.replace(sh_flag[0:1, :], offset=RV(spin_reg, min_val=0, max_val=48))
        spin_idx = len(OFF_TABLE)
        OFF_TABLE.append(("partner_flag", 16))
        def spin(e):
            with e.register("cnt") as cnt, e.register("g0") as g0, e.register("g1") as g1, \
                 e.register("df") as df, e.register("t0") as t0, e.register("t1") as t1, \
                 e.register("n0") as n0, e.register("n1") as n1:
                e.reg_load(n0, ms[0:1, 1:2])
                e.reg_load(n1, ms[0:1, 2:3])
                e.reg_load(spin_reg, offt[0:1, spin_idx:spin_idx + 1])
                e.reg_mov(cnt, 0)
                e.reg_mov(df, 1)
                with e.While(df):
                    e.dma_start(flg[0:1, :], pflag_ap).then_inc(spoll.h, 16)
                    e.reg_add(cnt, cnt, 16)
                    e.wait_ge(spoll.h, RV(cnt, min_val=0))
                    e.reg_load(g0, flg[0:1, 0:1])
                    e.reg_load(g1, flg[0:1, 1:2])
                    e.reg_alu(t0, g0, n0, ALU.bitwise_xor)
                    e.reg_alu(t1, g1, n1, ALU.bitwise_xor)
                    e.reg_alu(df, t0, t1, ALU.bitwise_or)
        B.ops["g"].append(spin)
        tk_hs = B.gd(aux2, sh_aux.rearrange("a b -> (a b)")[None, :], 16)
        B.v(lambda e: e.tensor_max(tmpr[0:1, 0:16], srow[0:1, 64:80], srow[0:1, 80:96]),
            waits=[tk_hs], inc=False)
        B.vgap()
        B.v(lambda e: e.tensor_mul(negU[0:1, :], tmpr[0:1, 0:16], qmax2[0:1, :]))
        tk_u1 = B.sV.tok()
        tk_u2 = B.a(lambda e: e.activation(negU[0:1, :], negU[0:1, :], AF.Sqrt), waits=[tk_u1])
        B.agap()
        tk_u3 = B.a(lambda e: e.activation(rowbf[0:1, 0:16], negU[0:1, :], AF.Identity, scale=-1.0),
                    waits=[tk_u2])
        tknm = B.mm(6, pb[6][:, 0:16], [(oner[:], rowbf[0:1, 0:16])], waits=[tk_u3])
        B.v(lambda e: e.tensor_copy(negUb[:], pb[6][:, 0:16]), waits=[tknm], reads=[6])
        tk_negub = B.sV.tok()
        if dbg:
            B.gd(dbg_out[11], sh_k[0, 0:128, 0:512], 2, waits=[tk_hs])
        if stage <= 2:
            return

        slot_free = {17: None, 18: None, 19: None}
        tk_outproj_done = None
        for g in range(4):
            tk_at = None
            for h in range(H):
                hp = h % 2
                lbank = 2 + hp
                abank = 6 + hp
                B.yd(kbuf[hp][:, 0:16, :], sh_k[0, h * P:(h + 1) * P, :]
                     .rearrange("p (a n) -> p a n", n=P), 17 + 10 * hp,
                     waits=[tk_hs, slot_free[17]])
                tk_kb = B.yd(kbuf[hp][:, 16:32, :], sh_k[1, h * P:(h + 1) * P, :]
                             .rearrange("p (a n) -> p a n", n=P), 17 + 10 * hp)
                tk_qb = B.yd(qbuf[hp], qt_scr[h * P:(h + 1) * P, g * 512:(g + 1) * 512],
                             18 + 3 * hp,
                             waits=[slot_free[18]])
                B.yd(vbuf[hp][:, 0:16, :], sh_v[0, :, :, h * P:(h + 1) * P]
                     .rearrange("a p d -> p a d"), 19 + 3 * hp,
                     waits=[tk_hs, slot_free[19]])
                tk_vb = B.yd(vbuf[hp][:, 16:32, :], sh_v[1, :, :, h * P:(h + 1) * P]
                             .rearrange("a p d -> p a d"), 19 + 3 * hp)
                tk_half_done = None
                for hftk in range(2):
                    for t2 in range(16):
                        tkt = hftk * 16 + t2
                        sb = tkt % 2
                        tks_ = B.mm(sb, pb[sb][:, :], [(kbuf[hp][:, tkt, :], qbuf[hp])],
                                    waits=[tk_kb, tk_qb])
                        B.a(lambda e, sb=sb, t2=t2, h=h: e.activation(
                            PT[:, t2, :], pb[sb][:, :], AF.Exp, bias=negUb[:, h:h + 1]),
                            waits=[tks_, tk_negub] +
                                  ([tk_half_done] if (t2 == 0 and tk_half_done) else []),
                            reads=[sb], inc=(t2 == 15))
                    tk_ptA = B.sA.tok()
                    for t2 in range(16):
                        gi = hftk * 16 + t2
                        B.mm(lbank, pb[lbank][0:1, :], [(onec[:], PT[:, t2, :])],
                             waits=[tk_ptA] if t2 == 0 else (),
                             first=(gi == 0), last=(gi == 31))
                    for t2 in range(16):
                        gi = hftk * 16 + t2
                        B.mm(abank, pb[abank][:, :], [(vbuf[hp][:, gi, :], PT[:, t2, :])],
                             waits=[tk_vb] if gi == 0 else (),
                             first=(gi == 0), last=(gi == 31))
                    tk_half_done = B.sT.tok()
                tk_l = tk_half_done
                tkrow = B.v(lambda e, lb=lbank: e.reciprocal(rl_f[0:1, :], pb[lb][0:1, :]),
                            waits=[tk_l], reads=[lbank])
                tkrow2 = B.a(lambda e: e.activation(rowbf, rl_f[0:1, :], AF.Identity),
                             waits=[tkrow])
                tkrb = B.mm(5, pb[5][:, :], [(oner[:], rowbf)], waits=[tkrow2])
                tk_rlb = B.a(lambda e: e.activation(rl_f[:], pb[5][:, :], AF.Identity),
                             waits=[tkrb] + ([tk_at] if tk_at else []), reads=[5])
                B.v(lambda e, ab=abank, h=h: e.tensor_mul(attnT[:, h, :], pb[ab][:, :], rl_f[:]),
                    waits=[tk_rlb] + ([tk_outproj_done] if (h == 0 and tk_outproj_done) else []),
                    reads=[abank])
                tk_at = B.sV.tok()
                if dbg and g == 0 and h == 0:
                    dbv = abf(36864, (512,))
                    B.v(lambda e: e.tensor_copy(dbv, PT[:, 0, :]), waits=[tk_at])
                    B.gd(dbg_out[0], dbv, 2, waits=[B.sV.tok()])
                    B.v(lambda e: e.tensor_copy(dbv, PT[:, 15, :]), waits=[B.dsem[2].tok()])
                    B.gd(dbg_out[1], dbv, 2, waits=[B.sV.tok()])
                    B.gd(dbg_out[2], rl_f[:], 2, waits=[B.sV.tok()])
                    B.v(lambda e: e.tensor_copy(dbv, attnT[:, 0, :]), waits=[B.dsem[2].tok()])
                    B.gd(dbg_out[3], dbv, 2, waits=[B.sV.tok()])
                    B.v(lambda e: e.tensor_copy(dbv, qbuf[hp]), waits=[B.dsem[2].tok()])
                    B.gd(dbg_out[4], dbv, 2, waits=[B.sV.tok()])
                    B.v(lambda e: e.tensor_copy(bass.AP(dbv.tensor, dbv.offset, [dbv.ap[0], [1, 128]]), kbuf[hp][:, 0, :]), waits=[B.dsem[2].tok()])
                    B.gd(dbg_out[5][:, 0:128], dbv[:, 0:128] if False else bass.AP(dbv.tensor, dbv.offset, [dbv.ap[0], [1, 128]]), 2, waits=[B.sV.tok()])
                    B.v(lambda e: e.tensor_copy(dbv[:, 0:16] if False else bass.AP(dbv.tensor, dbv.offset, [dbv.ap[0], [1, 16]]), negUb[:]), waits=[B.dsem[2].tok()])
                    B.gd(dbg_out[6][:, 0:16], bass.AP(dbv.tensor, dbv.offset, [dbv.ap[0], [1, 16]]), 2, waits=[B.sV.tok()])
                    B.v(lambda e: e.tensor_copy(bass.AP(dbv.tensor, dbv.offset, [dbv.ap[0], [1, 128]]), vbuf[hp][:, 0, :]), waits=[B.dsem[2].tok()])
                    B.gd(dbg_out[7][:, 0:128], bass.AP(dbv.tensor, dbv.offset, [dbv.ap[0], [1, 128]]), 2, waits=[B.sV.tok()])
                    slot_free[17] = B.dsem[2].tok()
                    slot_free[18] = slot_free[17]
                    slot_free[19] = slot_free[17]
                tkslot = B.sT.tok()
                slot_free[17] = tkslot
                slot_free[18] = tkslot
                slot_free[19] = tkslot
            if stage <= 3:
                continue
            tk_bout = B.gd(bout_row, bout_d[None, :], 23, waits=[tk_prev_use])
            tk_woprev = [None, None]
            for c in range(8):
                w = wob[c % 2]
                tk_w = B.gd(w, wout[:, c * 256:(c + 1) * 256].rearrange("(k p) n -> p k n", p=P),
                            21 + c % 2, waits=[tk_woprev[c % 2], tk_at] if tk_woprev[c % 2] else [tk_at])
                for i in range(4):
                    bank = (c * 4 + i) % 2
                    t = g * 4 + i
                    B.waits("t", [tk_w, tk_at, tk_bout, B.brd[bank]])
                    for kk in range(16):
                        B.ops["t"].append(
                            lambda e, o=pb[bank][:, 0:256],
                            l=attnT[:, kk, i * P:(i + 1) * P], r=w[:, kk, :], st=(kk == 0):
                            e.matmul(o, l, r, start=st, stop=False))
                    B.ops["t"].append(lambda e, o=pb[bank][:, 0:256], c=c:
                                      e.matmul(o, oner[:], bout_row[0:1, c * 256:(c + 1) * 256],
                                               start=False, stop=True).then_inc(B.sT.h, 1))
                    tkm = B.sT.inc(1)
                    B.bwr[bank] = tkm
                    tk_X[t] = B.v(lambda e, b=bank, t=t, c=c: e.tensor_tensor(
                        X[:, t, c * 256:(c + 1) * 256], X[:, t, c * 256:(c + 1) * 256],
                        pb[b][:, 0:256], ALU.add), waits=[tkm, tk_X[t]], reads=[bank])
                tk_woprev[c % 2] = B.sT.tok()
            tk_outproj_done = B.sT.tok()
            slot_free[17] = tk_outproj_done
            slot_free[18] = tk_outproj_done
            slot_free[19] = tk_outproj_done

    if 1 in subs:
        emit_attn_a()
    if 2 in subs:
        emit_ffn()
    if 3 in subs:
        emit_attn_b(stage=attnb_stage)
    if 4 in subs:
        emit_ffn()

    tk_out = B.yd(y_out.rearrange("(t p) d -> p t d", p=P), X[:], 1,
                  waits=[t for t in tk_X if t] + [tk_x])
    B.waits("y", [tk_out])

    with nc.Block() as block:
        @block.tensor
        def _(e):
            for f in B.ops["t"]:
                f(e)
        @block.vector
        def _(e):
            for f in B.ops["v"]:
                f(e)
        @block.scalar
        def _(e):
            for f in B.ops["a"]:
                f(e)
        @block.gpsimd
        def _(e):
            for f in B.ops["g"]:
                f(e)
        @block.sync
        def _(e):
            for f in B.ops["y"]:
                f(e)
    return nc


def make_in_maps(x, wq, wkv, wo, Wqkv, bqkv, Wout, bout, gate_w, w13, w2, sw13, sw2):
    wqT = np.ascontiguousarray(wq.T)
    wkvT = np.ascontiguousarray(wkv.T)
    woT = np.ascontiguousarray(wo.T)
    wqkv = np.ascontiguousarray(Wqkv)
    wout = np.ascontiguousarray(Wout)
    gateT = np.ascontiguousarray(gate_w.T)
    w13T = np.ascontiguousarray(w13.transpose(0, 2, 1))
    w2T = np.ascontiguousarray(w2.transpose(0, 2, 1))
    sw13T = np.ascontiguousarray(sw13.T)
    sw2T = np.ascontiguousarray(sw2.T)
    nonce = np.frombuffer(os.urandom(8), dtype=np.uint32) | np.uint32(1)
    in_maps = []
    for c in range(8):
        b, half = c // 2, c % 2
        mt = np.zeros((1, 16), dtype=np.uint32)
        mt[0, 0] = half
        mt[0, 1:3] = nonce
        fm = np.zeros((1, 4), dtype=BF)
        fm[0, 0] = 1.0 if half == 1 else 0.0
        offt = np.zeros((1, 192), dtype=np.uint32)
        for i, (S, sc) in enumerate(OFF_TABLE):
            if S == "partner_flag":
                offt[0, i] = (1 - half) * sc
            else:
                offt[0, i] = S + half * sc
        in_maps.append({
            "offt": offt,
            "x_loc": np.ascontiguousarray(x[b, half * 2048:(half + 1) * 2048]),
            "x_rem": np.ascontiguousarray(x[b, (1 - half) * 2048:(2 - half) * 2048]),
            "meta": mt, "fm_in": fm,
            "wqT": wqT, "wkvT": wkvT, "woT": woT,
            "wqkv": wqkv, "bqkv": np.ascontiguousarray(bqkv),
            "wout": wout, "bout": np.ascontiguousarray(bout),
            "gateT": gateT, "w13T": w13T, "w2T": w2T, "sw13T": sw13T, "sw2T": sw2T,
        })
    return in_maps


def kernel(L0_attn_a_x, L0_attn_a_W_q, L0_attn_a_W_kv, L0_attn_a_W_o,
           L0_attn_b_W_qkv, L0_attn_b_b_qkv, L0_attn_b_W_out, L0_attn_b_b_out,
           L0_ffn_gate_w, L0_ffn_w13, L0_ffn_w2, L0_ffn_sw13, L0_ffn_sw2):
    x = np.ascontiguousarray(L0_attn_a_x)
    nc = build_nc()
    in_maps = make_in_maps(x, L0_attn_a_W_q, L0_attn_a_W_kv, L0_attn_a_W_o,
                           L0_attn_b_W_qkv, L0_attn_b_b_qkv, L0_attn_b_W_out, L0_attn_b_b_out,
                           L0_ffn_gate_w, L0_ffn_w13, L0_ffn_w2, L0_ffn_sw13, L0_ffn_sw2)
    res = run_bass_kernel_spmd(nc, in_maps, core_ids=list(range(8)))
    out = np.empty((4, 4096, 2048), dtype=np.float32)
    for c in range(8):
        b, half = c // 2, c % 2
        out[b, half * 2048:(half + 1) * 2048] = res.results[c]["y"]
    return out


# version 04
# speedup vs baseline: 1.2059x; 1.2059x over previous
import os
import dataclasses
import numpy as np
import ml_dtypes
import concourse.bass as bass
import concourse.mybir as mybir
from concourse.bass_utils import run_bass_kernel_spmd
import concourse.bass_utils as _bass_utils

_orig_pnp = _bass_utils._process_ntff_profile
def _gc_pnp(*a, **kw):
    import gc
    gc.collect()
    return _orig_pnp(*a, **kw)
_bass_utils._process_ntff_profile = _gc_pnp
from concourse.expressions_rust import RuntimeValue as RV

F32 = mybir.dt.float32
BF16 = mybir.dt.bfloat16
U32 = mybir.dt.uint32
AF = mybir.ActivationFunctionType
ALU = mybir.AluOpType
AX = mybir.AxisListType
ds = bass.ds
BF = ml_dtypes.bfloat16

P = 128
D = 2048
NT = 16
H = 16
NE = 16
FF = 1408
NJ = FF // P
EPS = 1e-5
RS128 = float(1.0 / np.sqrt(128.0))
C0_KV = float(128.0 ** -1.5)
NEGBIG = -1.0e30


class Sem:
    def __init__(self, nc, name):
        self.h = nc.alloc_semaphore(name)
        self.n = 0
    def inc(self, k):
        self.n += k
        return (self, self.n)
    def tok(self):
        return (self, self.n)


class Builder:
    def __init__(self, nc):
        self.nc = nc
        self.ops = {k: [] for k in ("t", "v", "a", "g", "y")}
        self.last = {k: {} for k in ("t", "v", "a", "g", "y")}
        self.sT = Sem(nc, "sT")
        self.sV = Sem(nc, "sV")
        self.sA = Sem(nc, "sA")
        self.dsem = [Sem(nc, f"sd{i}") for i in range(30)]
        self.spoll = Sem(nc, "spoll")
        self.bwr = [None] * 8
        self.brd = [None] * 8

    def wait(self, key, tok):
        if tok is None:
            return
        sem, val = tok
        if val <= 0 or self.last[key].get(sem, -1) >= val:
            return
        self.last[key][sem] = val
        self.ops[key].append(lambda e, s=sem.h, v=val: e.wait_ge(s, v))

    def waits(self, key, toks):
        for t in toks:
            if t is not None:
                self.wait(key, t)

    def mm(self, bk, out_ap, pairs, waits=(), first=True, last=True):
        self.waits("t", list(waits) + ([self.brd[bk]] if first else []))
        n = len(pairs)
        for i, (l, r) in enumerate(pairs):
            st = first and (i == 0)
            sp = last and (i == n - 1)
            if sp:
                self.ops["t"].append(lambda e, o=out_ap, l=l, r=r, st=st:
                                     e.matmul(o, l, r, start=st, stop=True).then_inc(self.sT.h, 1))
            else:
                self.ops["t"].append(lambda e, o=out_ap, l=l, r=r, st=st, sp=sp:
                                     e.matmul(o, l, r, start=st, stop=sp))
        if last:
            tok = self.sT.inc(1)
            self.bwr[bk] = tok
            return tok
        return None

    def tr8(self, bk, src_list, dst_views, waits=()):
        self.waits("t", list(waits) + [self.brd[bk]])
        n = len(src_list)
        for i in range(n):
            if i == n - 1:
                self.ops["t"].append(lambda e, o=dst_views[i], s=src_list[i]:
                                     e.transpose(o, s, self.ident[:]).then_inc(self.sT.h, 1))
            else:
                self.ops["t"].append(lambda e, o=dst_views[i], s=src_list[i]:
                                     e.transpose(o, s, self.ident[:]))
        tok = self.sT.inc(1)
        self.bwr[bk] = tok
        return tok

    def v(self, fn, waits=(), reads=(), inc=True):
        if reads:
            inc = True
        self.waits("v", list(waits) + [self.bwr[b] for b in reads])
        if inc:
            self.ops["v"].append(lambda e, f=fn: f(e).then_inc(self.sV.h, 1))
            tok = self.sV.inc(1)
            for b in reads:
                self.brd[b] = tok
            return tok
        self.ops["v"].append(lambda e, f=fn: f(e))
        return None

    def a(self, fn, waits=(), reads=(), inc=True):
        if reads:
            inc = True
        self.waits("a", list(waits) + [self.bwr[b] for b in reads])
        if inc:
            self.ops["a"].append(lambda e, f=fn: f(e).then_inc(self.sA.h, 1))
            tok = self.sA.inc(1)
            for b in reads:
                self.brd[b] = tok
            return tok
        self.ops["a"].append(lambda e, f=fn: f(e))
        return None

    def vgap(self):
        self.ops["v"].append(lambda e: e.engine_nop())
        self.ops["v"].append(lambda e: e.engine_nop())

    def agap(self):
        self.ops["a"].append(lambda e: e.activation(self.adummy, self.adummy, AF.Identity))
        self.ops["a"].append(lambda e: e.activation(self.adummy, self.adummy, AF.Identity))

    def gd(self, dst, src, slot, waits=(), slow=False):
        self.waits("g", waits)
        sem = self.dsem[slot]
        self.ops["g"].append(lambda e, d=dst, s=src, sl=slow:
                             e.dma_start(d, s, allow_slow_non_contiguous=sl).then_inc(sem.h, 16))
        return sem.inc(16)

    def yd(self, dst, src, slot, waits=()):
        self.waits("y", waits)
        sem = self.dsem[slot]
        self.ops["y"].append(lambda e, d=dst, s=src: e.dma_start(d, s).then_inc(sem.h, 16))
        return sem.inc(16)


OFF_TABLE = []


def build_nc(subs=(1, 2, 3, 4), attnb_stage=4, dbg=False):
    OFF_TABLE.clear()
    nc = bass.Bass("TRN2", target_bir_lowering=False)
    B = Builder(nc)

    def din(name, shape, dt=F32):
        return nc.dram_tensor(name, list(shape), dt, kind="ExternalInput").ap()

    x_loc = din("x_loc", [2048, D])
    x_rem = din("x_rem", [2048, D])
    meta = din("meta", [1, 16], U32)
    fm_in = din("fm_in", [1, 4], BF16)
    offt_d = din("offt", [1, 192], U32)
    wqT = din("wqT", [D, D])
    wkvT = din("wkvT", [D, D])
    woT = din("woT", [D, D])
    wqkv = din("wqkv", [D, 3 * D])
    bqkv_d = din("bqkv", [3 * D])
    wout = din("wout", [D, D])
    bout_d = din("bout", [D])
    gateT = din("gateT", [D, NE])
    w13T = din("w13T", [NE, D, 2 * FF])
    w2T = din("w2T", [NE, FF, D])
    sw13T = din("sw13T", [D, 4 * FF])
    sw2T = din("sw2T", [2 * FF, D])
    y_out = nc.dram_tensor("y", [2048, D], F32, kind="ExternalOutput").ap()
    dbg_out = nc.dram_tensor("dbg", [12, P, 512], F32, kind="ExternalOutput").ap() if dbg else None

    qt_scr = nc.dram_tensor("qt_scr", [D, 2048], BF16, kind="Internal").ap()
    k_loc = nc.dram_tensor("k_loc", [D, 2048], BF16, kind="Internal").ap()
    v_loc = nc.dram_tensor("v_loc", [16, P, D], BF16, kind="Internal").ap()
    sh_k = nc.dram_tensor("sh_k", [2, D, 2048], BF16, kind="Internal", addr_space="Shared").ap()
    sh_v = nc.dram_tensor("sh_v", [2, 16, P, D], BF16, kind="Internal", addr_space="Shared").ap()
    sh_aux = nc.dram_tensor("sh_aux", [2, 16], F32, kind="Internal", addr_space="Shared").ap()
    sh_flag = nc.dram_tensor("sh_flag", [2, 16], U32, kind="Internal", addr_space="Shared").ap()

    r_ = np.arange(P)
    ang = 2.0 * np.pi * np.outer(r_, r_) / 128.0
    u2_np = np.zeros((P, 2, 256), dtype=BF)
    for j in range(2):
        for r in range(P):
            u2_np[r, j, (128 * j + r):] = 1.0
    c_idn = nc.inline_tensor(np.eye(P, dtype=BF), "c_idn")
    c_dftC = nc.inline_tensor(np.cos(ang).astype(BF), "c_dftC")
    c_dftS = nc.inline_tensor(np.sin(ang).astype(BF), "c_dftS")
    c_u2 = nc.inline_tensor(u2_np, "c_u2")
    c_onec = nc.inline_tensor(np.ones((P, 1), dtype=BF), "c_onec")
    c_eps = nc.inline_tensor(np.full((P, 1), EPS, dtype=np.float32), "c_eps")
    c_oner = nc.inline_tensor(np.ones((1, P), dtype=BF), "c_oner")

    Al = nc.alloc_sbuf_tensor
    X = Al("X", [P, NT, D], F32)
    ident = Al("ident", [P, P], BF16)
    B.ident = ident
    onec = Al("onec", [P, 1], BF16)
    oner = Al("oner", [1, P], BF16)
    gate_sb = Al("gate_sb", [P, 16, NE], BF16)
    bqkv_sb = Al("bqkv_sb", [P, 48], F32)
    bqkv_sc = Al("bqkv_sc", [P, 48], F32)
    ms = Al("ms", [1, 16], U32)
    flg = Al("flg", [1, 16], U32)
    fmb = Al("fmb", [1, 4], BF16)
    offt = Al("offt_sb", [1, 192], U32)
    fmcol = Al("fmcol", [P, 4], F32)
    stats = Al("stats", [P, 88], F32)
    epsc = Al("epsc", [P, 1], F32)
    adummy = Al("adummy", [1, 8], F32)
    B.adummy = adummy[0:1, 0:8]
    srow = Al("srow", [1, 96], F32)
    ARN = 39168
    arn = Al("arn", [P, ARN], BF16)

    def abf(off, shape):
        n = int(np.prod(shape))
        assert 0 <= off and off + n <= ARN, (off, shape)
        v = arn[:, off:off + n]
        if len(shape) > 1:
            names = " ".join(f"d{i}" for i in range(len(shape)))
            v = v.rearrange(f"p ({names}) -> p {names}",
                            **{f"d{i}": s for i, s in enumerate(shape)})
        return v

    def af32(off, shape):
        n = int(np.prod(shape))
        assert 0 <= off and off + 2 * n <= ARN, (off, shape)
        v = arn[:, off:off + 2 * n].bitcast(F32)
        if len(shape) > 1:
            names = " ".join(f"d{i}" for i in range(len(shape)))
            v = v.rearrange(f"p ({names}) -> p {names}",
                            **{f"d{i}": s for i, s in enumerate(shape)})
        return v

    def arow_bf(off, n):
        return arn[0:1, off:off + n]

    pb = [nc.alloc_psum_tensor(f"pb{i}", [P, 512], F32) for i in range(8)]
    def pbf_views(bk, n=8):
        v = pb[bk][:].bitcast(BF16).rearrange("p (c n) -> p c n", n=P)
        return [v[:, i, :] for i in range(n)]
    def pbf3(bk):
        return pb[bk][:].bitcast(BF16).rearrange("p (c n) -> p c n", n=P)

    tk_con = []
    tk_con.append(B.gd(ident[:], c_idn.ap(), 0))
    tk_con.append(B.gd(onec[:], c_onec.ap(), 0))
    tk_con.append(B.gd(oner[:], c_oner.ap(), 0))
    tk_con.append(B.gd(gate_sb[:], gateT.rearrange("(c p) n -> p c n", p=P), 0))
    tk_con.append(B.gd(bqkv_sb[:], bqkv_d.rearrange("(c p) -> p c", p=P), 0, slow=True))
    tk_con.append(B.gd(ms[:], meta, 0))
    tk_con.append(B.gd(fmb[:], fm_in, 0))
    tk_con.append(B.gd(epsc[:], c_eps.ap(), 0))
    tk_con.append(B.gd(offt[:], offt_d, 0))
    tk_con = [B.dsem[0].tok()]
    tk_x = B.yd(X[:], x_loc.rearrange("(t p) d -> p t d", p=P), 1)
    tkf = B.mm(7, pb[7][:, 0:4], [(oner[:], fmb[:])], waits=tk_con)
    B.v(lambda e: e.tensor_copy(fmcol[:], pb[7][:, 0:4]), waits=[tkf], reads=[7])
    tk_fm = B.sV.tok()
    for h in range(H):
        B.v(lambda e, h=h: e.tensor_scalar(bqkv_sc[:, 3 * h:3 * h + 1], bqkv_sb[:, 3 * h:3 * h + 1],
                                           RS128, None, ALU.mult),
            waits=tk_con if h == 0 else (), inc=(h == H - 1))
    B.v(lambda e: e.tensor_copy(
        bqkv_sc.rearrange("p (h c) -> p h c", c=3)[:, :, 1:3],
        bqkv_sb.rearrange("p (h c) -> p h c", c=3)[:, :, 1:3]))
    tk_bq = B.sV.tok()

    tk_X = [None] * NT

    def emit_ln(n, src, lnxT, scr0, scr1, tr_extra=()):
        s1 = stats[:, 0:n];        s2 = stats[:, n:2 * n]
        mn = stats[:, 2 * n:3 * n];  ex2 = stats[:, 3 * n:4 * n]
        m2 = stats[:, 4 * n:5 * n];  var = stats[:, 5 * n:6 * n]
        srt = stats[:, 6 * n:7 * n]; rstd = stats[:, 7 * n:8 * n]
        nb = stats[:, 8 * n:9 * n]
        dump = [scr0, scr1]
        aps, tks = [], []
        for i in range(n):
            ap, tk = src(i)
            aps.append(ap); tks.append(tk)
            B.v(lambda e, a=ap, o=s1[:, i:i + 1]: e.reduce_sum(o, a, axis=AX.X),
                waits=tk, inc=(i == n - 1))
        tk_s1 = B.sV.tok()
        for i in range(n):
            B.a(lambda e, a=aps[i], o=dump[i % 2], s=s2[:, i:i + 1]:
                e.activation(o, a, AF.Square, accum_out=s),
                waits=list(tks[i]) + list(tr_extra), inc=(i == n - 1))
        tk_s2 = B.sA.tok()
        B.v(lambda e: e.tensor_scalar(ex2[:], s2[:], 1.0 / D, None, ALU.mult),
            waits=[tk_s2], inc=False)
        B.v(lambda e: e.tensor_scalar(mn[:], s1[:], 1.0 / D, None, ALU.mult), waits=[tk_s1])
        tk_mn = B.sV.tok()
        tk_m2 = B.a(lambda e: e.activation(m2[:], mn[:], AF.Square), waits=[tk_mn])
        B.v(lambda e: e.tensor_sub(var[:], ex2[:], m2[:]), waits=[tk_m2])
        tk_var = B.sV.tok()
        tk_srt = B.a(lambda e: e.activation(srt[:], var[:], AF.Sqrt, bias=epsc[:, 0:1]),
                     waits=[tk_var] + tk_con)
        B.v(lambda e: e.reciprocal(rstd[:], srt[:]), waits=[tk_srt], inc=False)
        B.vgap()
        B.v(lambda e: e.scalar_tensor_tensor(nb[:], mn[:], -1.0, rstd[:], ALU.mult, ALU.mult))
        tk_nb = B.sV.tok()
        tk_vcp = None
        for i in range(n):
            lnx = dump[i % 2]
            tk_ap = B.a(lambda e, a=aps[i], o=lnx, i=i:
                        e.activation(o, a, AF.Identity, scale=rstd[:, i:i + 1], bias=nb[:, i:i + 1]),
                        waits=[tk_nb] + ([tk_vcp] if tk_vcp else []))
            lx2 = lnx.rearrange("p (c n) -> p c n", n=P)
            for hf in range(2):
                bank = 6 + hf
                tk_tr = B.tr8(bank, [lx2[:, hf * 8 + c, :] for c in range(8)], pbf_views(bank),
                              waits=[tk_ap] + list(tr_extra))
                tk_vcp = B.v(lambda e, b=bank, i=i, hf=hf: e.tensor_copy(
                    lnxT[:, hf * 8:(hf + 1) * 8, i * P:(i + 1) * P], pbf3(b)),
                    waits=[tk_tr], reads=[bank])
        return tk_vcp

    def xsrc_mk(gt):
        def f(i):
            t = gt[i]
            return (X[:, t, :], [tk_x, tk_X[t]])
        return f

    def emit_attn_a():
        o_lnxT, o_qnT, o_kvn = 0, 4096, 8192
        o_raw, o_sq, o_qn = 12288, 16384, 18432
        o_w = 22528
        o_u2, o_dC, o_dS = 30720, 31232, 31360
        o_cum, o_e = 31488, 32100

        lnxT = abf(o_lnxT, (16, 256))
        qnT = abf(o_qnT, (16, 256))
        kvn = abf(o_kvn, (2, 2048))
        raw = [abf(o_raw, (2048,)), abf(o_raw + 2048, (2048,))]
        sqb = abf(o_sq, (2048,))
        qnb = [abf(o_qn, (2048,)), abf(o_qn + 2048, (2048,))]
        wsl = [abf(o_w, (16, 256)), abf(o_w + 4096, (16, 256))]
        u2 = abf(o_u2, (2, 256))
        dC = abf(o_dC, (P,))
        dS = abf(o_dS, (P,))
        cums = [abf(o_cum, (256,)), abf(o_cum + 256, (256,))]
        kcs = abf(o_e, (256,)); kss = abf(o_e + 256, (256,))
        e1s = abf(o_e + 512, (256,)); e2s = abf(o_e + 768, (256,))
        t14 = [af32(o_e + 1024 + i * 512, (256,)) for i in range(4)]
        xr = [af32(4096, (2048,)), af32(18432, (2048,))]

        tk_cst = [B.gd(u2, c_u2.ap(), 2), B.gd(dC, c_dftC.ap(), 2), B.gd(dS, c_dftS.ap(), 2)]

        carry = [stats[:, 18 + 16 * i:34 + 16 * i] for i in range(2)]
        s2h = stats[:, 50:66]
        rch = stats[:, 66:82]
        rch_b = bass.AP(rch.tensor, rch.offset, [rch.ap[0], [rch.ap[1][0], 16], [0, P]])

        wtok = [None, None]

        def proj_norm(WT, tk_ln, c0, dst, pre=()):
            tk_ev = [None, None]
            for c in range(8):
                w = wsl[c % 2]
                tk_w = B.gd(w, WT[:, c * 256:(c + 1) * 256].rearrange("(k p) n -> p k n", p=P),
                            3 if c % 2 == 0 else 9,
                            waits=[wtok[c % 2]] if wtok[c % 2] else ())
                for i in range(2):
                    bank = (c * 2 + i) % 2
                    tkm = B.mm(bank, pb[bank][:, 0:256],
                               [(lnxT[:, k, i * P:(i + 1) * P], w[:, k, :]) for k in range(16)],
                               waits=[tk_w, tk_ln])
                    tk_ev[i] = B.a(lambda e, b=bank, i=i, c=c:
                                   e.activation(raw[i][:, c * 256:(c + 1) * 256],
                                                pb[b][:, 0:256], AF.Identity),
                                   waits=[tkm] + (list(pre) if c == 0 else []), reads=[bank])
                wtok[c % 2] = B.sT.tok()
            out = []
            for i in range(2):
                B.agap()
                tk_sq = B.a(lambda e, i=i: e.activation(sqb, raw[i], AF.Square), waits=[tk_ev[i]])
                B.v(lambda e: e.reduce_sum(s2h[:], sqb.rearrange("p (h n) -> p h n", n=P),
                                           axis=AX.X), waits=[tk_sq])
                tk_rt = B.a(lambda e: e.activation(rch[:], s2h[:], AF.Sqrt), waits=[B.sV.tok()])
                B.v(lambda e: e.reciprocal(rch[:], rch[:]), waits=[tk_rt], inc=False)
                B.vgap()
                B.v(lambda e, i=i, d=dst[i]: e.scalar_tensor_tensor(
                    d.rearrange("p (h n) -> p h n", n=P),
                    raw[i].rearrange("p (h n) -> p h n", n=P), c0, rch_b, ALU.mult, ALU.mult))
                out.append(B.sV.tok())
            return out

        tk_sum = None
        first = True
        for g in range(8):
            tk_src = []
            for i in range(2):
                t = 2 * g + i
                tk_src.append([B.yd(xr[i][:], x_rem.rearrange("(t p) d -> p t d", p=P)[:, t, :],
                                    4 if i == 0 else 15,
                                    waits=[tk_sum] if tk_sum else ())])
            tk_ln = emit_ln(2, lambda i: (xr[i], tk_src[i]), lnxT, raw[0], raw[1],
                            tr_extra=tk_cst if g == 0 else ())
            tk_kvn = proj_norm(wkvT, tk_ln, C0_KV, [kvn[:, 0, :], kvn[:, 1, :]])
            for i in range(2):
                for h in range(H):
                    B.mm(5, pb[5][:, h:h + 1],
                         [(kvn[:, i, :].rearrange("p (hh n) -> p hh n", n=P)[:, h, :], onec[:])],
                         waits=[tk_kvn[i]] if h == 0 else (),
                         first=first, last=(g == 7 and i == 1 and h == H - 1))
                    first = False
            tk_sum = B.sT.tok()
        B.v(lambda e: e.tensor_scalar(carry[0][:], pb[5][:, 0:16], fmcol[:, 0:1], None, ALU.mult),
            waits=[tk_sum, tk_fm], reads=[5])
        tk_carry = B.sV.tok()

        tk_prev_grp = None
        for g in range(8):
            gt = [2 * g, 2 * g + 1]
            cpar = g % 2
            tk_ln = emit_ln(2, xsrc_mk(gt), lnxT, raw[0], raw[1],
                            tr_extra=[tk_prev_grp] if tk_prev_grp else ())
            tk_qn = proj_norm(wqT, tk_ln, RS128, [qnb[0], qnb[1]])
            tk_qT = None
            for i in range(2):
                q2 = qnb[i].rearrange("p (c n) -> p c n", n=P)
                for hf in range(2):
                    bank = 6 + hf
                    tk_tr = B.tr8(bank, [q2[:, hf * 8 + c, :] for c in range(8)], pbf_views(bank),
                                  waits=[tk_qn[i]])
                    tk_qT = B.v(lambda e, b=bank, i=i, hf=hf: e.tensor_copy(
                        qnT[:, hf * 8:(hf + 1) * 8, i * P:(i + 1) * P], pbf3(b)),
                        waits=[tk_tr], reads=[bank])
            tk_kvn = proj_norm(wkvT, tk_ln, C0_KV, [kvn[:, 0, :], kvn[:, 1, :]], pre=tk_qn)
            vhT = lnxT
            tk_vh = None
            tk_cu_use = [None, None]
            tk_e_prev = None
            for h in range(H):
                kv_h = [kvn[:, j, :].rearrange("p (hh n) -> p hh n", n=P)[:, h, :] for j in range(2)]
                tkc = B.mm(2, pb[2][:, 0:256], [(kv_h[j], u2[:, j, :]) for j in range(2)],
                           waits=[tk_kvn[0], tk_kvn[1], tk_carry])
                cu = cums[h % 2]
                tk_cume = B.a(lambda e, cu=cu, h=h, cp=cpar:
                              e.activation(cu, pb[2][:, 0:256], AF.Identity,
                                           bias=carry[cp][:, h:h + 1]),
                              waits=[tkc] + ([tk_cu_use[h % 2]] if tk_cu_use[h % 2] else []),
                              reads=[2])
                B.v(lambda e, h=h, cp=cpar: e.tensor_tensor(
                    carry[1 - cp][:, h:h + 1], pb[2][:, 255:256], carry[cp][:, h:h + 1], ALU.add),
                    waits=[tkc, tk_cume], reads=[2])
                tkq1 = B.mm(0, pb[0][:, 0:256], [(dC, qnT[:, h, :])], waits=[tk_qT])
                tkq2 = B.mm(1, pb[1][:, 0:256], [(dS, qnT[:, h, :])])
                tkk1 = B.mm(3, pb[3][:, 0:256], [(dC, cu)], waits=[tk_cume])
                tkk2 = B.mm(4, pb[4][:, 0:256], [(dS, cu)])
                tk_cu_use[h % 2] = tkk2
                tk_kc = B.a(lambda e: e.activation(kcs, pb[3][:, 0:256], AF.Identity),
                            waits=[tkk1] + ([tk_e_prev] if tk_e_prev else []), reads=[3])
                tk_ks = B.a(lambda e: e.activation(kss, pb[4][:, 0:256], AF.Identity),
                            waits=[tkk2], reads=[4])
                B.v(lambda e: e.tensor_mul(t14[0][:], pb[0][:, 0:256], kcs),
                    waits=[tkq1, tk_kc], inc=False)
                B.v(lambda e: e.tensor_mul(t14[1][:], pb[1][:, 0:256], kss),
                    waits=[tkq2, tk_ks], inc=False)
                B.v(lambda e: e.tensor_mul(t14[2][:], pb[0][:, 0:256], kss), inc=False)
                B.v(lambda e: e.tensor_mul(t14[3][:], pb[1][:, 0:256], kcs), inc=False)
                B.v(lambda e: e.tensor_sub(e1s, t14[0][:], t14[1][:]), inc=False)
                B.v(lambda e: e.tensor_add(e2s, t14[2][:], t14[3][:]))
                tk_e = B.sV.tok()
                tk_e_prev = tk_e
                B.brd[0] = tk_e; B.brd[1] = tk_e
                tk_vhm = B.mm(5, pb[5][:, 0:256], [(dC, e1s), (dS, e2s)], waits=[tk_e])
                tk_vh = B.v(lambda e, h=h, vhT=vhT: e.tensor_copy(vhT[:, h, :], pb[5][:, 0:256]),
                            waits=[tk_vhm], reads=[5])
            for c in range(8):
                w = wsl[c % 2]
                tk_w = B.gd(w, woT[:, c * 256:(c + 1) * 256].rearrange("(k p) n -> p k n", p=P),
                            3 if c % 2 == 0 else 9,
                            waits=[wtok[c % 2]] if wtok[c % 2] else ())
                for i in range(2):
                    bank = (c * 2 + i) % 2
                    tkm = B.mm(bank, pb[bank][:, 0:256],
                               [(vhT[:, k, i * P:(i + 1) * P], w[:, k, :]) for k in range(16)],
                               waits=[tk_w, tk_vh])
                    t = gt[i]
                    tk_X[t] = B.v(lambda e, b=bank, t=t, c=c: e.tensor_tensor(
                        X[:, t, c * 256:(c + 1) * 256], X[:, t, c * 256:(c + 1) * 256],
                        pb[b][:, 0:256], ALU.add), waits=[tkm, tk_X[t]], reads=[bank])
                wtok[c % 2] = B.sT.tok()
            tk_prev_grp = B.sT.tok()

    ffn_state = {"tk_h_prev": None, "tk_w2_slot": [None, None], "tk_w13_slot": [None, None]}

    def emit_ffn():
        o_lnxT, o_cg, o_hT = 0, 16384, 16640
        o_wk, o_sil, o_wA = 27904, 28416, 29440
        o_w2s1, o_wB = 27904, 33536
        lnxT = abf(o_lnxT, (16, 1024))
        wk = [af32(o_wk, (8, 16)), af32(o_wk + 256, (8, 16))]
        cg = af32(o_cg, (8, 16))
        sil = [abf(o_sil, (512,)), abf(o_sil + 512, (512,))]
        hT = abf(o_hT, (NJ, 1024))
        w13s = [abf(o_wA, (16, P)), abf(o_wA + 2048, (16, P))]
        w2s = [abf(o_wB, (NJ, 512)), abf(o_w2s1, (NJ, 512))]
        scr0 = abf(o_hT, (2048,))
        scr1 = abf(o_hT + 2048, (2048,))
        gth = stats[:, 72:80]
        gtmp = stats[:, 80:88]

        ent = B.sT.tok()
        tk_h_prev = ent
        tk_w2_slot = [ent, ent]
        tk_w13_slot = [ent, ent]

        for grp in range(2):
            gt = list(range(grp * 8, grp * 8 + 8))
            tk_ln = emit_ln(8, xsrc_mk(gt), lnxT, scr0, scr1,
                            tr_extra=[tk_h_prev] if tk_h_prev else ())
            for i in range(8):
                bank = 6 + i % 2
                tkg = B.mm(bank, pb[bank][:, 0:16],
                           [(lnxT[:, k, i * P:(i + 1) * P], gate_sb[:, k, :]) for k in range(16)],
                           waits=[tk_ln])
                B.v(lambda e, b=bank, i=i: e.reduce_max(gtmp[:, i:i + 1], pb[b][:, 0:16],
                                                        axis=AX.X, negate=True),
                    waits=[tkg], inc=False)
                B.vgap()
                B.v(lambda e, b=bank, i=i: e.tensor_scalar(wk[0][:, i, :], pb[b][:, 0:16],
                                                           gtmp[:, i:i + 1], None, ALU.add),
                    reads=[bank])
                tk_sc = B.sV.tok()
                B.a(lambda e, i=i: e.activation(wk[1][:, i, :], wk[0][:, i, :], AF.Exp),
                    waits=[tk_sc], inc=(i == 7))
            tk_exp = B.sA.tok()
            for i in range(8):
                B.v(lambda e, i=i: e.reduce_sum(gtmp[:, i:i + 1], wk[1][:, i, :], axis=AX.X),
                    waits=[tk_exp] if i == 0 else (), inc=(i == 7))
            for i in range(8):
                B.v(lambda e, i=i: e.reciprocal(gth[:, i:i + 1], gtmp[:, i:i + 1]), inc=(i == 7))
            for i in range(8):
                B.v(lambda e, i=i: e.tensor_scalar(wk[1][:, i, :], wk[1][:, i, :], gth[:, i:i + 1],
                                                   None, ALU.mult), inc=(i == 7))
            for i in range(8):
                B.v(lambda e, i=i: e.tensor_copy(wk[0][:, i, :], wk[1][:, i, :]), inc=(i == 7))
            for it in range(6):
                for i in range(8):
                    B.v(lambda e, i=i: e.reduce_max(gtmp[:, i:i + 1], wk[0][:, i, :], axis=AX.X),
                        inc=(i == 7))
                if it == 5:
                    break
                for i in range(8):
                    B.v(lambda e, i=i: e.tensor_scalar(cg[:, i, :], wk[0][:, i, :],
                                                       gtmp[:, i:i + 1], None, ALU.is_ge),
                        inc=(i == 7))
                for i in range(8):
                    B.v(lambda e, i=i: e.scalar_tensor_tensor(wk[0][:, i, :], cg[:, i, :], NEGBIG,
                                                              wk[0][:, i, :], ALU.mult, ALU.add),
                        inc=(i == 7))
            for i in range(8):
                B.v(lambda e, i=i: e.tensor_scalar(cg[:, i, :], wk[1][:, i, :], gtmp[:, i:i + 1],
                                                   None, ALU.is_ge), inc=(i == 7))
            for i in range(8):
                B.v(lambda e, i=i: e.tensor_mul(cg[:, i, :], cg[:, i, :], wk[1][:, i, :]),
                    inc=(i == 7))
            for i in range(8):
                B.v(lambda e, i=i: e.reduce_sum(gth[:, i:i + 1], cg[:, i, :], axis=AX.X),
                    inc=(i == 7))
            for i in range(8):
                B.v(lambda e, i=i: e.tensor_scalar(gtmp[:, i:i + 1], gth[:, i:i + 1], 1e-20, None,
                                                   ALU.add), inc=(i == 7))
            for i in range(8):
                B.v(lambda e, i=i: e.reciprocal(gth[:, i:i + 1], gtmp[:, i:i + 1]), inc=(i == 7))
            for i in range(8):
                B.v(lambda e, i=i: e.tensor_scalar(cg[:, i, :], cg[:, i, :], gth[:, i:i + 1],
                                                   None, ALU.mult), inc=(i == 7))
            tk_cg = B.sV.tok()

            for ep in range(NE + 2):
                if ep < NE:
                    w13g = lambda j, ep=ep: w13T[ep, :, j * P:(j + 1) * P]
                    w13u = lambda j, ep=ep: w13T[ep, :, FF + j * P:FF + (j + 1) * P]
                    w2src = lambda dc, ep=ep: w2T[ep, :, dc * 512:(dc + 1) * 512]
                    def c_ap(tt, ep=ep):
                        return cg[:, tt, ep:ep + 1]
                else:
                    sp = ep - NE
                    w13g = lambda j, sp=sp: sw13T[:, sp * FF + j * P:sp * FF + (j + 1) * P]
                    w13u = lambda j, sp=sp: sw13T[:, 2 * FF + sp * FF + j * P:2 * FF + sp * FF + (j + 1) * P]
                    w2src = lambda dc, sp=sp: sw2T[sp * FF:(sp + 1) * FF, dc * 512:(dc + 1) * 512]
                    c_ap = None
                tk_h = [None] * NJ
                tk_hmul = [None, None]
                for j in range(NJ):
                    tk_g = B.gd(w13s[0], w13g(j).rearrange("(k p) n -> p k n", p=P), 5,
                                waits=[tk_w13_slot[0]] if tk_w13_slot[0] else ())
                    tk_u = B.gd(w13s[1], w13u(j).rearrange("(k p) n -> p k n", p=P), 6,
                                waits=[tk_w13_slot[1]] if tk_w13_slot[1] else ())
                    for hf in range(2):
                        tkg = B.mm(hf, pb[hf][:, :],
                                   [(w13s[0][:, k, :], lnxT[:, k, hf * 512:(hf + 1) * 512])
                                    for k in range(16)], waits=[tk_g, tk_ln])
                        tku = B.mm(2 + hf, pb[2 + hf][:, :],
                                   [(w13s[1][:, k, :], lnxT[:, k, hf * 512:(hf + 1) * 512])
                                    for k in range(16)], waits=[tk_u])
                        tk_si = B.a(lambda e, hf=hf: e.activation(sil[hf], pb[hf][:, :], AF.Silu),
                                    waits=[tkg] + ([tk_hmul[hf]] if tk_hmul[hf] else []),
                                    reads=[hf])
                        tk_hmul[hf] = B.v(lambda e, j=j, hf=hf: e.tensor_mul(
                            hT[:, j, hf * 512:(hf + 1) * 512], sil[hf], pb[2 + hf][:, :]),
                            waits=[tk_si, tku] + ([tk_h_prev] if tk_h_prev else []),
                            reads=[2 + hf])
                        tk_h[j] = tk_hmul[hf]
                    tk_w13_slot = [B.sT.tok(), B.sT.tok()]
                tk_w2_slot[0] = B.gd(w2s[0], w2src(0).rearrange("(j p) n -> p j n", p=P), 7,
                                     waits=[tk_w2_slot[0]] if tk_w2_slot[0] else ())
                for dc in range(4):
                    sl = dc % 2
                    if dc > 0:
                        w = [tk_w2_slot[sl]] if tk_w2_slot[sl] else []
                        if sl == 1 and dc == 1:
                            w += [tk_w13_slot[0], tk_w13_slot[1]]
                        tk_w2_slot[sl] = B.gd(w2s[sl], w2src(dc).rearrange("(j p) n -> p j n", p=P),
                                              7 + sl, waits=w)
                    for tt in range(8):
                        bank = 4 + tt % 2
                        tkb = B.mm(bank, pb[bank][:, :],
                                   [(hT[:, j, tt * P:(tt + 1) * P], w2s[sl][:, j, :])
                                    for j in range(NJ)],
                                   waits=[tk_w2_slot[sl]] + (tk_h if (dc == 0 and tt == 0) else []))
                        t = gt[tt]
                        if c_ap is None:
                            tk_X[t] = B.v(lambda e, b=bank, t=t, dc=dc: e.tensor_tensor(
                                X[:, t, dc * 512:(dc + 1) * 512], X[:, t, dc * 512:(dc + 1) * 512],
                                pb[b][:, :], ALU.add), waits=[tkb, tk_X[t]], reads=[bank])
                        else:
                            tk_X[t] = B.v(lambda e, b=bank, t=t, dc=dc, ca=c_ap(tt):
                                          e.scalar_tensor_tensor(
                                X[:, t, dc * 512:(dc + 1) * 512], pb[b][:, :], ca,
                                X[:, t, dc * 512:(dc + 1) * 512], ALU.mult, ALU.add),
                                waits=[tkb, tk_X[t], tk_cg], reads=[bank])
                    tk_w2_slot[sl] = B.sT.tok()
                tk_h_prev = B.sT.tok()
                tk_w13_slot = [tk_h_prev, tk_h_prev]
        ffn_state["tk_h_prev"] = tk_h_prev
        ffn_state["tk_w2_slot"] = tk_w2_slot
        ffn_state["tk_w13_slot"] = tk_w13_slot

    def emit_attn_b(stage=4):
        lnxT = abf(0, (16, 512))
        wqs = [abf(8192, (16, 3, P)), abf(14336, (16, 3, P))]
        qst = [abf(20480, (512,)), abf(20992, (512,))]
        kst = [abf(21504, (512,)), abf(22016, (512,))]
        ksq = abf(22528, (512,))
        vst = abf(23040, (512,))
        vts = [abf(23552, (4, P)), abf(24064, (4, P))]
        scr0 = abf(8192, (2048,))
        scr1 = abf(10240, (2048,))
        kbuf = [abf(8192, (32, P)), abf(12288, (32, P))]
        qbuf = [abf(16384, (512,)), abf(16896, (512,))]
        vbuf = [abf(17408, (32, P)), abf(21504, (32, P))]
        PT = abf(25600, (16, 512))
        attnT = abf(0, (16, 512))
        wob = [abf(25600, (16, 256)), abf(29696, (16, 256))]
        rl_f = af32(33792, (512,))
        rowbf = arow_bf(34816, 512)
        bout_row = arow_bf(35328, 2048)
        negUb = stats[:, 36:52]
        qmax2 = srow[0:1, 0:16]
        kmax2 = srow[0:1, 16:32]
        negU = srow[0:1, 32:48]
        tmpr = srow[0:1, 48:64]
        aux2 = srow[0:1, 64:96]

        B.waits("g", tk_con)

        def gd_dyn(static_ap, half_scale, src, slot, waits=(), small=False):
            B.waits("g", waits)
            sem = B.dsem[slot]
            idx = len(OFF_TABLE)
            OFF_TABLE.append((S_HS := (int(static_ap.offset), half_scale)))
            reg = nc.gpsimd.alloc_register(f"dynoff{idx}")
            rv = RV(reg, min_val=0, max_val=48 if small else 2 * 16 * P * D)
            ap2 = dataclasses.replace(static_ap, offset=rv)
            def op(e, d=ap2, s=src, reg=reg, idx=idx):
                e.reg_load(reg, offt[0:1, idx:idx + 1])
                e.dma_start(d, s).then_inc(sem.h, 16)
            B.ops["g"].append(op)
            return sem.inc(16)

        B.v(lambda e: e.memset(qmax2, 0.0), inc=False)
        B.v(lambda e: e.memset(kmax2, 0.0))
        tk_mx0 = B.sV.tok()

        tk_q_dma = tk_k_dma = tk_v_dma = None
        ent_b = B.sT.tok()
        tk_wprev = [ent_b, ent_b]
        tk_prev_use = ent_b
        for g in range(4):
            gt = list(range(g * 4, g * 4 + 4))
            tk_ln = emit_ln(4, xsrc_mk(gt), lnxT, scr0, scr1,
                            tr_extra=[tk_prev_use] if tk_prev_use else ())
            for hb in range(H):
                hp = hb % 2
                w = wqs[hp]
                tk_w = B.gd(w, wqkv[:, hb * 384:(hb + 1) * 384]
                            .rearrange("(k p) (c n) -> p k c n", p=P, n=P),
                            10 if hp == 0 else 20,
                            waits=[tk_wprev[hp], tk_ln] if tk_wprev[hp] else [tk_ln])
                tkq = B.mm(0, pb[0][:, :], [(w[:, k, 0, :], lnxT[:, k, :]) for k in range(16)],
                           waits=[tk_w, tk_ln, tk_q_dma])
                tk_qe = B.a(lambda e, hb=hb, hp=hp: e.activation(qst[hp], pb[0][:, :], AF.Identity,
                            scale=RS128, bias=bqkv_sc[:, 3 * hb:3 * hb + 1]),
                            waits=[tkq, tk_bq], reads=[0])
                B.agap()
                tk_qsq = B.a(lambda e, hp=hp: e.activation(ksq, qst[hp], AF.Square), waits=[tk_qe])
                tkqs = B.mm(3, pb[3][0:1, :], [(onec[:], ksq)], waits=[tk_qsq])
                B.v(lambda e: e.reduce_max(tmpr[0:1, 0:1], pb[3][0:1, :], axis=AX.X),
                    waits=[tkqs], inc=False)
                B.vgap()
                B.v(lambda e, hb=hb: e.tensor_max(qmax2[0:1, hb:hb + 1], qmax2[0:1, hb:hb + 1],
                                                  tmpr[0:1, 0:1]), waits=[tk_mx0], reads=[3])
                tk_q_dma = B.yd(qt_scr[hb * P:(hb + 1) * P, g * 512:(g + 1) * 512], qst[hp],
                                11 if hp == 0 else 24, waits=[B.sA.tok()])
                tkk = B.mm(1, pb[1][:, :], [(w[:, k, 1, :], lnxT[:, k, :]) for k in range(16)],
                           waits=[tk_k_dma])
                tk_ke = B.a(lambda e, hb=hb, hp=hp: e.activation(kst[hp], pb[1][:, :], AF.Identity,
                            bias=bqkv_sc[:, 3 * hb + 1:3 * hb + 2]), waits=[tkk], reads=[1])
                B.agap()
                tk_ksq = B.a(lambda e, hp=hp: e.activation(ksq, kst[hp], AF.Square), waits=[tk_ke])
                tkks = B.mm(3, pb[3][0:1, :], [(onec[:], ksq)], waits=[tk_ksq])
                B.v(lambda e: e.reduce_max(tmpr[0:1, 1:2], pb[3][0:1, :], axis=AX.X),
                    waits=[tkks], inc=False)
                B.vgap()
                B.v(lambda e, hb=hb: e.tensor_max(kmax2[0:1, hb:hb + 1], kmax2[0:1, hb:hb + 1],
                                                  tmpr[0:1, 1:2]), reads=[3])
                tk_k_dma = B.yd(k_loc[hb * P:(hb + 1) * P, g * 512:(g + 1) * 512],
                                kst[hp], 12 if hp == 0 else 25, waits=[B.sA.tok()])
                if dbg and g == 0 and hb == 0:
                    B.gd(dbg_out[8], sh_k[0, 0:128, 0:512], 2, waits=[tk_k_dma])
                    B.gd(dbg_out[9], kst[hp], 2, waits=[B.dsem[2].tok()])
                    B.gd(dbg_out[10][0:1, 0:192], offt[:], 2, waits=[B.dsem[2].tok()])
                tkv = B.mm(2, pb[2][:, :], [(w[:, k, 2, :], lnxT[:, k, :]) for k in range(16)],
                           waits=[tk_v_dma])
                tk_ve = B.a(lambda e, hb=hb: e.activation(vst, pb[2][:, :], AF.Identity,
                            bias=bqkv_sc[:, 3 * hb + 2:3 * hb + 3]), waits=[tkv], reads=[2])
                v2 = vst.rearrange("p (c n) -> p c n", n=P)
                tk_vt = B.tr8(5, [v2[:, tt, :] for tt in range(4)], pbf_views(5, 4),
                              waits=[tk_ve])
                tk_vc = B.v(lambda e, hp=hp: e.tensor_copy(vts[hp][:], pbf3(5)[:, 0:4, :]),
                            waits=[tk_vt], reads=[5])
                tk_v_dma = B.yd(
                    v_loc[g * 4:(g + 1) * 4, :, hb * P:(hb + 1) * P]
                    .rearrange("a p d -> p a d"),
                    vts[hp], 13 if hp == 0 else 26, waits=[tk_vc])
                tk_wprev[hp] = B.sT.tok()
            tk_prev_use = B.sT.tok()

        if stage <= 1:
            return
        tk_pub_k = gd_dyn(sh_k[0, :, :], D * 2048, k_loc[:, :], 12,
                          waits=[B.dsem[12].tok(), B.dsem[25].tok()])
        tk_pub_v = gd_dyn(sh_v[0, :, :, :].rearrange("a p d -> (a p) d"), 16 * P * D,
                          v_loc[:, :, :].rearrange("a p d -> (a p) d"), 13,
                          waits=[B.dsem[13].tok(), B.dsem[26].tok()])
        tk_aux = gd_dyn(sh_aux[0:1, :], 16, kmax2, 14,
                        waits=[B.sV.tok(), tk_pub_k, tk_pub_v,
                               B.dsem[11].tok(), B.dsem[24].tok()], small=True)
        B.ops["g"].append(lambda e: e.memset(flg[:], 0))
        B.ops["g"].append(lambda e: e.tensor_copy(flg[0:1, 0:2], ms[0:1, 1:3]))
        B.waits("g", [tk_aux])
        tk_flag = gd_dyn(sh_flag[0:1, :], 16, flg[0:1, :], 14, small=True)
        B.waits("g", [tk_flag])
        spoll = B.spoll

        spin_reg = nc.gpsimd.alloc_register("spinoff")
        pflag_ap = dataclasses.replace(sh_flag[0:1, :], offset=RV(spin_reg, min_val=0, max_val=48))
        spin_idx = len(OFF_TABLE)
        OFF_TABLE.append(("partner_flag", 16))
        def spin(e):
            with e.register("cnt") as cnt, e.register("g0") as g0, e.register("g1") as g1, \
                 e.register("df") as df, e.register("t0") as t0, e.register("t1") as t1, \
                 e.register("n0") as n0, e.register("n1") as n1:
                e.reg_load(n0, ms[0:1, 1:2])
                e.reg_load(n1, ms[0:1, 2:3])
                e.reg_load(spin_reg, offt[0:1, spin_idx:spin_idx + 1])
                e.reg_mov(cnt, 0)
                e.reg_mov(df, 1)
                with e.While(df):
                    e.dma_start(flg[0:1, :], pflag_ap).then_inc(spoll.h, 16)
                    e.reg_add(cnt, cnt, 16)
                    e.wait_ge(spoll.h, RV(cnt, min_val=0))
                    e.reg_load(g0, flg[0:1, 0:1])
                    e.reg_load(g1, flg[0:1, 1:2])
                    e.reg_alu(t0, g0, n0, ALU.bitwise_xor)
                    e.reg_alu(t1, g1, n1, ALU.bitwise_xor)
                    e.reg_alu(df, t0, t1, ALU.bitwise_or)
        B.ops["g"].append(spin)
        tk_hs = B.gd(aux2, sh_aux.rearrange("a b -> (a b)")[None, :], 16)
        B.v(lambda e: e.tensor_max(tmpr[0:1, 0:16], srow[0:1, 64:80], srow[0:1, 80:96]),
            waits=[tk_hs], inc=False)
        B.vgap()
        B.v(lambda e: e.tensor_mul(negU[0:1, :], tmpr[0:1, 0:16], qmax2[0:1, :]))
        tk_u1 = B.sV.tok()
        tk_u2 = B.a(lambda e: e.activation(negU[0:1, :], negU[0:1, :], AF.Sqrt), waits=[tk_u1])
        B.agap()
        tk_u3 = B.a(lambda e: e.activation(rowbf[0:1, 0:16], negU[0:1, :], AF.Identity, scale=-1.0),
                    waits=[tk_u2])
        tknm = B.mm(6, pb[6][:, 0:16], [(oner[:], rowbf[0:1, 0:16])], waits=[tk_u3])
        B.v(lambda e: e.tensor_copy(negUb[:], pb[6][:, 0:16]), waits=[tknm], reads=[6])
        tk_negub = B.sV.tok()
        if dbg:
            B.gd(dbg_out[11], sh_k[0, 0:128, 0:512], 2, waits=[tk_hs])
        if stage <= 2:
            return

        slot_free = {17: None, 18: None, 19: None}
        tk_outproj_done = None
        for g in range(4):
            tk_at = None
            for h in range(H):
                hp = h % 2
                lbank = 2 + hp
                abank = 6 + hp
                B.yd(kbuf[hp][:, 0:16, :], sh_k[0, h * P:(h + 1) * P, :]
                     .rearrange("p (a n) -> p a n", n=P), 17 + 10 * hp,
                     waits=[tk_hs, slot_free[17]])
                tk_kb = B.yd(kbuf[hp][:, 16:32, :], sh_k[1, h * P:(h + 1) * P, :]
                             .rearrange("p (a n) -> p a n", n=P), 17 + 10 * hp)
                tk_qb = B.yd(qbuf[hp], qt_scr[h * P:(h + 1) * P, g * 512:(g + 1) * 512],
                             18 + 3 * hp,
                             waits=[slot_free[18]])
                B.yd(vbuf[hp][:, 0:16, :], sh_v[0, :, :, h * P:(h + 1) * P]
                     .rearrange("a p d -> p a d"), 19 + 3 * hp,
                     waits=[tk_hs, slot_free[19]])
                tk_vb = B.yd(vbuf[hp][:, 16:32, :], sh_v[1, :, :, h * P:(h + 1) * P]
                             .rearrange("a p d -> p a d"), 19 + 3 * hp)
                tk_half_done = None
                for hftk in range(2):
                    for t2 in range(16):
                        tkt = hftk * 16 + t2
                        sb = tkt % 2
                        tks_ = B.mm(sb, pb[sb][:, :], [(kbuf[hp][:, tkt, :], qbuf[hp])],
                                    waits=[tk_kb, tk_qb])
                        B.a(lambda e, sb=sb, t2=t2, h=h: e.activation(
                            PT[:, t2, :], pb[sb][:, :], AF.Exp, bias=negUb[:, h:h + 1]),
                            waits=[tks_, tk_negub] +
                                  ([tk_half_done] if (t2 == 0 and tk_half_done) else []),
                            reads=[sb], inc=(t2 == 15))
                    tk_ptA = B.sA.tok()
                    for t2 in range(16):
                        gi = hftk * 16 + t2
                        B.mm(lbank, pb[lbank][0:1, :], [(onec[:], PT[:, t2, :])],
                             waits=[tk_ptA] if t2 == 0 else (),
                             first=(gi == 0), last=(gi == 31))
                    for t2 in range(16):
                        gi = hftk * 16 + t2
                        B.mm(abank, pb[abank][:, :], [(vbuf[hp][:, gi, :], PT[:, t2, :])],
                             waits=[tk_vb] if gi == 0 else (),
                             first=(gi == 0), last=(gi == 31))
                    tk_half_done = B.sT.tok()
                tk_l = tk_half_done
                tkrow = B.v(lambda e, lb=lbank: e.reciprocal(rl_f[0:1, :], pb[lb][0:1, :]),
                            waits=[tk_l], reads=[lbank])
                tkrow2 = B.a(lambda e: e.activation(rowbf, rl_f[0:1, :], AF.Identity),
                             waits=[tkrow])
                tkrb = B.mm(5, pb[5][:, :], [(oner[:], rowbf)], waits=[tkrow2])
                tk_rlb = B.a(lambda e: e.activation(rl_f[:], pb[5][:, :], AF.Identity),
                             waits=[tkrb] + ([tk_at] if tk_at else []), reads=[5])
                B.v(lambda e, ab=abank, h=h: e.tensor_mul(attnT[:, h, :], pb[ab][:, :], rl_f[:]),
                    waits=[tk_rlb] + ([tk_outproj_done] if (h == 0 and tk_outproj_done) else []),
                    reads=[abank])
                tk_at = B.sV.tok()
                if dbg and g == 0 and h == 0:
                    dbv = abf(36864, (512,))
                    B.v(lambda e: e.tensor_copy(dbv, PT[:, 0, :]), waits=[tk_at])
                    B.gd(dbg_out[0], dbv, 2, waits=[B.sV.tok()])
                    B.v(lambda e: e.tensor_copy(dbv, PT[:, 15, :]), waits=[B.dsem[2].tok()])
                    B.gd(dbg_out[1], dbv, 2, waits=[B.sV.tok()])
                    B.gd(dbg_out[2], rl_f[:], 2, waits=[B.sV.tok()])
                    B.v(lambda e: e.tensor_copy(dbv, attnT[:, 0, :]), waits=[B.dsem[2].tok()])
                    B.gd(dbg_out[3], dbv, 2, waits=[B.sV.tok()])
                    B.v(lambda e: e.tensor_copy(dbv, qbuf[hp]), waits=[B.dsem[2].tok()])
                    B.gd(dbg_out[4], dbv, 2, waits=[B.sV.tok()])
                    B.v(lambda e: e.tensor_copy(bass.AP(dbv.tensor, dbv.offset, [dbv.ap[0], [1, 128]]), kbuf[hp][:, 0, :]), waits=[B.dsem[2].tok()])
                    B.gd(dbg_out[5][:, 0:128], dbv[:, 0:128] if False else bass.AP(dbv.tensor, dbv.offset, [dbv.ap[0], [1, 128]]), 2, waits=[B.sV.tok()])
                    B.v(lambda e: e.tensor_copy(dbv[:, 0:16] if False else bass.AP(dbv.tensor, dbv.offset, [dbv.ap[0], [1, 16]]), negUb[:]), waits=[B.dsem[2].tok()])
                    B.gd(dbg_out[6][:, 0:16], bass.AP(dbv.tensor, dbv.offset, [dbv.ap[0], [1, 16]]), 2, waits=[B.sV.tok()])
                    B.v(lambda e: e.tensor_copy(bass.AP(dbv.tensor, dbv.offset, [dbv.ap[0], [1, 128]]), vbuf[hp][:, 0, :]), waits=[B.dsem[2].tok()])
                    B.gd(dbg_out[7][:, 0:128], bass.AP(dbv.tensor, dbv.offset, [dbv.ap[0], [1, 128]]), 2, waits=[B.sV.tok()])
                    slot_free[17] = B.dsem[2].tok()
                    slot_free[18] = slot_free[17]
                    slot_free[19] = slot_free[17]
                tkslot = B.sT.tok()
                slot_free[17] = tkslot
                slot_free[18] = tkslot
                slot_free[19] = tkslot
            if stage <= 3:
                continue
            tk_bout = B.gd(bout_row, bout_d[None, :], 23, waits=[tk_prev_use])
            tk_woprev = [None, None]
            for c in range(8):
                w = wob[c % 2]
                tk_w = B.gd(w, wout[:, c * 256:(c + 1) * 256].rearrange("(k p) n -> p k n", p=P),
                            21 + c % 2, waits=[tk_woprev[c % 2], tk_at] if tk_woprev[c % 2] else [tk_at])
                for i in range(4):
                    bank = (c * 4 + i) % 2
                    t = g * 4 + i
                    B.waits("t", [tk_w, tk_at, tk_bout, B.brd[bank]])
                    for kk in range(16):
                        B.ops["t"].append(
                            lambda e, o=pb[bank][:, 0:256],
                            l=attnT[:, kk, i * P:(i + 1) * P], r=w[:, kk, :], st=(kk == 0):
                            e.matmul(o, l, r, start=st, stop=False))
                    B.ops["t"].append(lambda e, o=pb[bank][:, 0:256], c=c:
                                      e.matmul(o, oner[:], bout_row[0:1, c * 256:(c + 1) * 256],
                                               start=False, stop=True).then_inc(B.sT.h, 1))
                    tkm = B.sT.inc(1)
                    B.bwr[bank] = tkm
                    tk_X[t] = B.v(lambda e, b=bank, t=t, c=c: e.tensor_tensor(
                        X[:, t, c * 256:(c + 1) * 256], X[:, t, c * 256:(c + 1) * 256],
                        pb[b][:, 0:256], ALU.add), waits=[tkm, tk_X[t]], reads=[bank])
                tk_woprev[c % 2] = B.sT.tok()
            tk_outproj_done = B.sT.tok()
            slot_free[17] = tk_outproj_done
            slot_free[18] = tk_outproj_done
            slot_free[19] = tk_outproj_done

    if 1 in subs:
        emit_attn_a()
    if 2 in subs:
        emit_ffn()
    if 3 in subs:
        emit_attn_b(stage=attnb_stage)
    if 4 in subs:
        emit_ffn()

    tk_out = B.yd(y_out.rearrange("(t p) d -> p t d", p=P), X[:], 1,
                  waits=[t for t in tk_X if t] + [tk_x])
    B.waits("y", [tk_out])

    with nc.Block() as block:
        @block.tensor
        def _(e):
            for f in B.ops["t"]:
                f(e)
        @block.vector
        def _(e):
            for f in B.ops["v"]:
                f(e)
        @block.scalar
        def _(e):
            for f in B.ops["a"]:
                f(e)
        @block.gpsimd
        def _(e):
            for f in B.ops["g"]:
                f(e)
        @block.sync
        def _(e):
            for f in B.ops["y"]:
                f(e)
    return nc


def make_in_maps(x, wq, wkv, wo, Wqkv, bqkv, Wout, bout, gate_w, w13, w2, sw13, sw2):
    wqT = np.ascontiguousarray(wq.T)
    wkvT = np.ascontiguousarray(wkv.T)
    woT = np.ascontiguousarray(wo.T)
    wqkv = np.ascontiguousarray(Wqkv)
    wout = np.ascontiguousarray(Wout)
    gateT = np.ascontiguousarray(gate_w.T)
    w13T = np.ascontiguousarray(w13.transpose(0, 2, 1))
    w2T = np.ascontiguousarray(w2.transpose(0, 2, 1))
    sw13T = np.ascontiguousarray(sw13.T)
    sw2T = np.ascontiguousarray(sw2.T)
    nonce = np.frombuffer(os.urandom(8), dtype=np.uint32) | np.uint32(1)
    in_maps = []
    for c in range(8):
        b, half = c // 2, c % 2
        mt = np.zeros((1, 16), dtype=np.uint32)
        mt[0, 0] = half
        mt[0, 1:3] = nonce
        fm = np.zeros((1, 4), dtype=BF)
        fm[0, 0] = 1.0 if half == 1 else 0.0
        offt = np.zeros((1, 192), dtype=np.uint32)
        for i, (S, sc) in enumerate(OFF_TABLE):
            if S == "partner_flag":
                offt[0, i] = (1 - half) * sc
            else:
                offt[0, i] = S + half * sc
        in_maps.append({
            "offt": offt,
            "x_loc": np.ascontiguousarray(x[b, half * 2048:(half + 1) * 2048]),
            "x_rem": np.ascontiguousarray(x[b, (1 - half) * 2048:(2 - half) * 2048]),
            "meta": mt, "fm_in": fm,
            "wqT": wqT, "wkvT": wkvT, "woT": woT,
            "wqkv": wqkv, "bqkv": np.ascontiguousarray(bqkv),
            "wout": wout, "bout": np.ascontiguousarray(bout),
            "gateT": gateT, "w13T": w13T, "w2T": w2T, "sw13T": sw13T, "sw2T": sw2T,
        })
    return in_maps


def kernel(L0_attn_a_x, L0_attn_a_W_q, L0_attn_a_W_kv, L0_attn_a_W_o,
           L0_attn_b_W_qkv, L0_attn_b_b_qkv, L0_attn_b_W_out, L0_attn_b_b_out,
           L0_ffn_gate_w, L0_ffn_w13, L0_ffn_w2, L0_ffn_sw13, L0_ffn_sw2):
    x = np.ascontiguousarray(L0_attn_a_x)
    nc = build_nc()
    in_maps = make_in_maps(x, L0_attn_a_W_q, L0_attn_a_W_kv, L0_attn_a_W_o,
                           L0_attn_b_W_qkv, L0_attn_b_b_qkv, L0_attn_b_W_out, L0_attn_b_b_out,
                           L0_ffn_gate_w, L0_ffn_w13, L0_ffn_w2, L0_ffn_sw13, L0_ffn_sw2)
    res = run_bass_kernel_spmd(nc, in_maps, core_ids=list(range(8)))
    out = np.empty((4, 4096, 2048), dtype=np.float32)
    for c in range(8):
        b, half = c // 2, c % 2
        out[b, half * 2048:(half + 1) * 2048] = res.results[c]["y"]
    return out
